# version 34
# baseline (speedup 1.0000x reference)
import contextlib
import math

import numpy as np

import concourse.bass as bass
import concourse.mybir as mybir
from concourse.bass_utils import run_bass_kernel_spmd
from concourse.alu_op_type import AluOpType as OP

F32 = mybir.dt.float32
AF = mybir.ActivationFunctionType
AX = mybir.AxisListType

B, T, C = 4, 4096, 1024
H, DH = 16, 64
NBLK = 32
NOWN = 16
TOWN = NOWN * 128
NCC = 8
NEXP = 8
CCAP = 384
SLOPES = [2.0 ** (-(h + 1) / 2.0) for h in range(H)]
BIG = 1e30

ENGS = ("te", "ve", "se", "sy", "gp")
STREAMS = {"sW": "sy", "sC": "sy", "gX": "gp", "gI": "gp", "gS": "gp"}
SEMS = ENGS + tuple(STREAMS)


class _Buf:
    __slots__ = ("w", "readers")

    def __init__(self):
        self.w = None
        self.readers = {}


class _Prog:
    def __init__(self):
        self.ops = {e: [] for e in ENGS}
        self.cnt = {s: 0 for s in SEMS}
        self.waited = {e: {} for e in ENGS}

    SELF_MARGIN = 6

    def emit(self, eng, fn, r=(), w=(), step=1, stream=None, war_only=False):
        tag = stream or eng
        waits = {}

        def need(dep, val):
            if dep == eng and dep in ("te", "ve", "se"):
                if self.cnt[eng] - val >= self.SELF_MARGIN:
                    return
            if dep in STREAMS:
                val = self.cnt[dep]
            if self.waited[eng].get(dep, 0) >= val:
                return
            if waits.get(dep, 0) >= val:
                return
            assert val <= self.cnt[dep], (eng, dep, val, self.cnt[dep])
            waits[dep] = val

        for buf in r:
            if buf.w is not None:
                need(*buf.w)
        for buf in w:
            if buf.w is not None and not war_only:
                need(*buf.w)
            for de, dv in buf.readers.items():
                need(de, dv)
        self.cnt[tag] += step
        after = self.cnt[tag]
        for de, dv in waits.items():
            self.waited[eng][de] = max(self.waited[eng].get(de, 0), dv)
        self.ops[eng].append((sorted(waits.items()), fn, step, tag))
        for buf in r:
            buf.readers[tag] = after
        for buf in w:
            buf.w = (tag, after)
            buf.readers = {}

    def barrier(self):
        snap = dict(self.cnt)
        for eng in ENGS:
            waits = []
            for de in SEMS:
                if de == eng and de in ("te", "ve", "se"):
                    continue
                v = snap[de]
                if v and self.waited[eng].get(de, 0) < v:
                    waits.append((de, v))
                    self.waited[eng][de] = v
            if waits:
                self.ops[eng].append((waits, None, 0, eng))


def _build(taps=()):
    nc = bass.Bass("TRN2", target_bir_lowering=False)

    def din(name, shape):
        return nc.dram_tensor(name, shape, F32, kind="ExternalInput").ap()

    x = din("x", [T, C])
    wqT = din("wqT", [C, C])
    wkT = din("wkT", [C, C])
    wvT = din("wvT", [C, C])
    woT = din("woT", [C, C])
    gwT = din("gwT", [C, NEXP])
    w1 = din("w1", [NEXP, C, C])
    w2 = din("w2", [NEXP, C, C])
    w3 = din("w3", [NEXP, C, C])
    b2 = din("b2", [NEXP, C])
    b1c = din("b1c", [128, 64])
    b3c = din("b3c", [128, 64])
    sw1 = din("sw1", [C, C])
    sw2 = din("sw2", [C, C])
    sw3 = din("sw3", [C, C])
    sb1c = din("sb1c", [128, 8])
    sb2c = din("sb2c", [128, 8])
    sb3c = din("sb3c", [128, 8])
    ident = din("ident", [128, 128])
    bco = din("bco", [1, 128])
    onescol = din("onescol", [128, 1])
    alibi_in = din("alibi", [128, 512])
    maska_in = din("maska", [128, 128])
    maskb_in = din("maskb", [128, 128])
    iota_in = din("iota", [1, TOWN])
    ctri_in = din("ctri", [128, 144])
    ci128_in = din("ci128", [128, 1])
    cones2_in = din("cones2", [128, 2])
    cslopes_in = din("cslopes", [2, NEXP])
    csel2_in = din("csel2", [2, 256])

    y = nc.dram_tensor("y", [TOWN, C], F32, kind="ExternalOutput").ap()
    xnf = nc.dram_tensor("xnf", [C, T], F32,
                         kind="ExternalOutput" if "xnf" in taps else "Internal").ap()
    xntd = nc.dram_tensor("xnt", [TOWN, C], F32).ap()
    combd = nc.dram_tensor("combd", [TOWN + 512, NEXP], F32,
                           kind="ExternalOutput" if "combd" in taps else "Internal").ap()
    listsd = nc.dram_tensor("listsd", [2 * NEXP * 640, 1], mybir.dt.uint32,
                            kind="ExternalOutput" if "listsd" in taps else "Internal").ap()
    yffn = nc.dram_tensor("yffn", [TOWN + CCAP + 8, C], F32,
                          kind="ExternalOutput" if "yffn" in taps else "Internal").ap()
    tapd = {}
    for tname, shape in (("kph0", [128, T]), ("kpair", [128, T]), ("qpair", [128, TOWN]),
                         ("vtiles", [128, 4160]), ("oall", [C, TOWN]),
                         ("accp3", [C, TOWN]), ("probe", [128, 512]),
                         ("comb", [8, TOWN])):
        if tname in taps:
            tapd[tname] = nc.dram_tensor("tap_" + tname, shape, F32,
                                         kind="ExternalOutput").ap()

    P = _Prog()
    es = contextlib.ExitStack()
    with es:
        sem = {e: es.enter_context(nc.semaphore("s_" + e)) for e in SEMS}

        def sb(name, shape):
            return es.enter_context(nc.sbuf_tensor(name, shape, F32))

        tO = sb("tO", [128, 16384])
        tXA = sb("tXA", [128, 16384])
        tK = sb("tK", [128, 4096])
        tV = sb("tV", [128, 4160])
        tQ = sb("tQ", [128, 2048])
        tP = sb("tP", [128, 2048])
        tSH = sb("tSH", [128, 1024])
        tW = sb("tW", [128, 3072])
        tM = sb("tM", [128, 192])
        cID = sb("cID", [128, 128])
        cBCO = sb("cBCO", [1, 128])
        cONE = sb("cONE", [128, 1])
        cALI = sb("cALI", [128, 512])
        cMA = sb("cMA", [128, 128])
        cMB = sb("cMB", [128, 128])
        cB1 = sb("cB1", [128, 64])
        cB3 = sb("cB3", [128, 64])
        cSB1 = sb("cSB1", [128, 8])
        cSB2 = sb("cSB2", [128, 8])
        cSB3 = sb("cSB3", [128, 8])
        cGW = sb("cGW", [128, 64])
        cTRI = sb("cTRI", [128, 144])
        cI128 = sb("cI128", [128, 1])
        cON2 = sb("cON2", [128, 2])
        cSLP = sb("cSLP", [2, NEXP])
        cSL2 = sb("cSL2", [2, 256])
        cSENT = es.enter_context(nc.sbuf_tensor("cSENT", [128, 5], mybir.dt.uint32))

        PS = [es.enter_context(nc.psum_tensor("ps%d" % i, [128, 1024], F32))
              for i in range(4)]

        OB = [_Buf() for _ in range(8)]
        XB = [_Buf() for _ in range(4)]
        KB = [_Buf() for _ in range(4)]
        VB = [_Buf() for _ in range(4)]
        QB = [_Buf() for _ in range(2)]
        PB = [_Buf() for _ in range(4)]
        SHB = [_Buf() for _ in range(2)]
        WB = [_Buf() for _ in range(3)]
        MB = _Buf()
        CB = _Buf()
        PSB = [_Buf() for _ in range(8)]
        XNF = _Buf()
        XNT = _Buf()
        CMD = _Buf()
        LB = [_Buf() for _ in range(NEXP)]
        PFH = [[_Buf() for _ in range(NEXP)] for _ in range(2)]
        YFB = _Buf()

        def bank(i):
            return PS[i // 2][:, (i % 2) * 512:(i % 2 + 1) * 512]

        def ob(c0, c1):
            return tuple(OB[c0 // 2048:(c1 - 1) // 2048 + 1])

        def dma(q, dst, src, r, w, stream=None, war_only=False):
            if stream is None:
                stream = "sW" if q == "sy" else "gX"
            P.emit(q, lambda e, a=dst, b=src: e.dma_start(out=a, in_=b),
                   r=r, w=w, step=16, stream=stream, war_only=war_only)

        def idma_gather(out, in_, idx, r, w):
            P.emit("gp", lambda e, o=out, i=in_, x=idx:
                   e.indirect_dma_start(out=o, out_offset=None, in_=i,
                                        in_offset=bass.IndirectOffsetOnAxis(ap=x, axis=0)),
                   r=r, w=w, step=16, stream="gI")

        def idma_scatter(out, in_, idx, r, w, accum=False, war_only=False):
            kw = {"compute_op": OP.add} if accum else {}
            P.emit("gp", lambda e, o=out, i=in_, x=idx, k=kw:
                   e.indirect_dma_start(out=o, out_offset=bass.IndirectOffsetOnAxis(ap=x, axis=0),
                                        in_=i, in_offset=None, **k),
                   r=r, w=w, step=16, stream="gI", war_only=war_only)

        def mm(out, lhsT, rhs, r, w, start=True, stop=True):
            P.emit("te",
                   lambda e, o=out, l=lhsT, m=rhs, st=start, sp=stop:
                   e.matmul(o, l, m, start=st, stop=sp, skip_group_check=True),
                   r=r, w=w)

        def tr(out, in_, idn, r, w, start=True):
            P.emit("te", lambda e, o=out, i=in_, d=idn, s=start:
                   e.matmul(o, i, d, start=s, stop=True, is_transpose=True,
                            skip_group_check=True), r=r, w=w)

        def ve_copy(out, in_, r, w):
            P.emit("ve", lambda e, o=out, i=in_: e.tensor_copy(o, i), r=r, w=w)

        def ve_tt(out, in0, in1, op, r, w):
            P.emit("ve", lambda e, o=out, a=in0, b=in1, p=op: e.tensor_tensor(o, a, b, p),
                   r=r, w=w)

        def ve_ts(out, in0, s1, op0, r, w, s2=None, op1=None):
            if op1 is None:
                P.emit("ve", lambda e, o=out, a=in0, s=s1, p=op0:
                       e.tensor_scalar(o, a, s, None, p), r=r, w=w)
            else:
                P.emit("ve", lambda e, o=out, a=in0, s=s1, t=s2, p=op0, q=op1:
                       e.tensor_scalar(o, a, s, t, p, q), r=r, w=w)

        def ve_stt(out, in0, s, in1, op0, op1, r, w):
            P.emit("ve", lambda e, o=out, a=in0, c=s, b=in1, p=op0, q=op1:
                   e.scalar_tensor_tensor(o, a, c, b, p, q), r=r, w=w)

        def ve_red(out, in_, op, r, w, absv=None, neg=None):
            if op == "max":
                P.emit("ve", lambda e, o=out, i=in_, a=absv, n=neg:
                       e.reduce_max(o, i, axis=AX.X, apply_absolute_value=a, negate=n),
                       r=r, w=w)
            else:
                P.emit("ve", lambda e, o=out, i=in_: e.reduce_sum(o, i, axis=AX.X),
                       r=r, w=w)

        def ve_recip(out, in_, r, w):
            P.emit("ve", lambda e, o=out, i=in_: e.reciprocal(o, i), r=r, w=w)

        def ve_memset(ap, const, w):
            P.emit("ve", lambda e, a=ap, c=const: e.memset(a, c), r=(), w=w)

        def act(out, in_, func, r, w, bias=None, scale=None):
            kw = {}
            if bias is not None:
                kw["bias"] = bias
            if scale is not None:
                kw["scale"] = scale
            P.emit("se", lambda e, o=out, i=in_, f=func, k=kw: e.activation(o, i, f, **k),
                   r=r, w=w)

        for dst, src in ((cID, ident), (cBCO, bco), (cONE, onescol), (cALI, alibi_in),
                         (cMA, maska_in), (cMB, maskb_in),
                         (cB1, b1c), (cB3, b3c), (cSB1, sb1c), (cSB2, sb2c),
                         (cSB3, sb3c), (cTRI, ctri_in), (cI128, ci128_in),
                         (cON2, cones2_in), (cSL2, csel2_in), (cSLP, cslopes_in)):
            dma("sy", dst[:, :], src, r=(), w=(CB,), stream="sC")
        dma("sy", cGW[:, :].rearrange("p (c g) -> p c g", g=8),
            gwT.rearrange("(c p) g -> p c g", p=128), r=(), w=(CB,), stream="sC")

        padf = tM[:, 160:161]
        P.emit("ve", lambda e: e.tensor_scalar(padf, cI128[:, :], float(CCAP + 1), None, OP.add),
               r=(CB,), w=(MB,))
        for c5 in range(5):
            P.emit("ve", lambda e, c=c5: e.tensor_scalar(tM[:, 161 + c:162 + c], cI128[:, :],
                                                         5.0, float(TOWN + c), OP.mult, OP.add),
                   r=(CB,), w=(MB,))
        ve_copy(cSENT[:, :], tM[:, 161:166], r=(MB,), w=(CB,))
        zb = tO[:, 0:1024]
        ve_memset(zb, 0.0, w=(OB[0],))
        for zi in range((TOWN + CCAP + 8) // 128 + 1):
            r0 = zi * 128
            r1 = min(r0 + 128, TOWN + CCAP + 8)
            if r0 >= r1:
                break
            dma("sy", yffn[r0:r1, :], zb[0:r1 - r0, :], r=(OB[0],), w=(YFB,), stream="sC")
        for zi in range(4):
            dma("sy", combd[TOWN + zi * 128: TOWN + (zi + 1) * 128, :], zb[:, 0:8],
                r=(OB[0],), w=(CMD,), stream="sC")

        x4 = [(tK[:, 0:4096], tuple(KB)), (tV[:, 0:4096], tuple(VB))]
        sq4 = (tO[:, 10240:14336], ob(10240, 14336))
        stage = tO[:, 2048:10240]
        stB = ob(2048, 10240)
        for grp in range(NBLK // 4):
            sl = grp % 2
            xb, xbB = x4[sl]
            sq_, sqB = sq4
            for k in range(4):
                tt = grp * 4 + k
                dma("gp", xb[:, k * 1024:(k + 1) * 1024], x[tt * 128:(tt + 1) * 128, :],
                    r=xbB, w=xbB)
            s4 = tM[:, 20 + sl * 16: 24 + sl * 16]
            mean4 = tM[:, 24 + sl * 16: 28 + sl * 16]
            var4 = tM[:, 28 + sl * 16: 32 + sl * 16]
            rst4 = tM[:, 32 + sl * 16: 36 + sl * 16]

            def s4v(ap):
                return ap.rearrange("p (t o) -> p t o", o=1)

            def x4v(ap):
                return ap.rearrange("p (t c) -> p t c", c=C)

            P.emit("ve", lambda e, a=s4v(s4), b=x4v(xb): e.tensor_reduce(a, b, axis=AX.X,
                                                                         op=OP.add),
                   r=xbB, w=(MB,))
            ve_ts(mean4, s4, 1.0 / C, OP.mult, r=(MB,), w=(MB,))
            P.emit("ve", lambda e, a=x4v(xb), m=s4v(mean4).broadcast_to((128, 4, C)):
                   e.tensor_tensor(a, a, m, OP.subtract), r=xbB + (MB,), w=xbB)
            act(sq_, xb, AF.Square, r=xbB + sqB, w=sqB)
            P.emit("ve", lambda e, a=s4v(s4), b=x4v(sq_): e.tensor_reduce(a, b, axis=AX.X,
                                                                          op=OP.add),
                   r=sqB, w=(MB,))
            ve_ts(var4, s4, 1.0 / C, OP.mult, r=(MB,), w=(MB,), s2=1e-5, op1=OP.add)
            act(var4, var4, AF.Sqrt, r=(MB,), w=(MB,))
            ve_recip(rst4, var4, r=(MB,), w=(MB,))
            P.emit("ve", lambda e, a=x4v(xb), m=s4v(rst4).broadcast_to((128, 4, C)):
                   e.tensor_tensor(a, a, m, OP.mult), r=xbB + (MB,), w=xbB)
            for k in range(4):
                tt = grp * 4 + k
                xn_ = xb[:, k * 1024:(k + 1) * 1024]
                ps = PS[tt % 2]
                psb = (PSB[2 * (tt % 2)], PSB[2 * (tt % 2) + 1])
                for cc in range(NCC):
                    tr(ps[:, cc * 128:(cc + 1) * 128], xn_[:, cc * 128:(cc + 1) * 128],
                       cID[:, :], r=xbB + (CB,), w=(psb[cc // 4],), start=(cc % 4 == 0))
                k8 = tt % 8
                stv = stage.rearrange("p (c t) -> p c t", t=1024)[:, :, k8 * 128:(k8 + 1) * 128]
                P.emit("ve", lambda e, o=stv, i=ps[:, :1024].rearrange("p (c t) -> p c t", t=128):
                       e.tensor_copy(o, i), r=psb + stB, w=stB)
                if tt % 2 == 0:
                    jown = tt // 2
                    dma("gp", xntd[jown * 128:(jown + 1) * 128, :], xn_, r=xbB, w=(XNT,),
                        stream="gS", war_only=True)
            if grp % 2 == 1:
                g8 = grp // 2
                for cc in range(NCC):
                    dma("gp", xnf[cc * 128:(cc + 1) * 128, g8 * 1024:(g8 + 1) * 1024],
                        stage[:, cc * 1024:(cc + 1) * 1024], r=stB, w=(XNF,),
                        stream="gS", war_only=True)
        P.barrier()

        for sc in range(32):
            ve_memset(tV[:, sc * 130 + 64: sc * 130 + 65], 1.0, w=(VB[(sc * 130) // 1040],))
            ve_memset(tV[:, sc * 130 + 129: sc * 130 + 130], 1.0, w=(VB[(sc * 130 + 129) // 1040],))

        kpair = tK[:, :]
        kpB = KB
        qpair = tQ[:, :]
        qpB = QB

        def vtile(sc):
            return tV[:, sc * 130:(sc + 1) * 130]

        def vtB(sc):
            return VB[(sc * 130) // 1040]

        xach = [tXA[:, i * 4096:(i + 1) * 4096] for i in range(4)]
        sqa = tXA[:, 0:4096]
        normq2 = tXA[0:1, 4096:6144]
        shiftrow = [tXA[0:1, 6144:8192], tXA[0:1, 8192:10240]]
        normk2 = tXA[0:1, 10240:14336]
        iotarow = tXA[0:1, 10240:12288]
        sloperow = tXA[0:1, 12288:14336]
        maxk2 = tM[0:1, 130:131]
        maxk8 = tM[0:1, 131:132]
        opair = tM[:, 0:128]
        recc = [tM[:, 128:129], tM[:, 129:130]]
        wqb = (tW[:, 0:1024], WB[0])
        wkb = (tW[:, 1024:2048], WB[1])
        wvb = (tW[:, 2048:3072], WB[2])
        pbuf = [[(tP[:, 0:512], PB[0]), (tP[:, 512:1024], PB[1])],
                [(tP[:, 1024:1536], PB[2]), (tP[:, 1536:2048], PB[3])]]
        shbc = [(tSH[:, 0:512], SHB[0]), (tSH[:, 512:1024], SHB[1])]

        for pair in range(8):
            for wb_, wsrc in ((wqb, wqT), (wkb, wkT), (wvb, wvT)):
                dma("sy", wb_[0].rearrange("p (c d) -> p c d", d=128),
                    wsrc[:, pair * 128:(pair + 1) * 128].rearrange("(c p) d -> p c d", p=128),
                    r=(wb_[1],), w=(wb_[1],))
            for hidx2, half in enumerate(range(2)):
                iord = (2, 3, 0, 1) if half == 0 else (0, 1, 2, 3)
                for i in range(4):
                    cc = half * 4 + i
                    if half == 0 and pair > 0 and i >= 2:
                        continue
                    dma("gp", xach[i], xnf[cc * 128:(cc + 1) * 128, :], r=(XB[i], XNF), w=(XB[i],))
                for ii, i in enumerate(iord):
                    for scp in range(8):
                        mm(bank(scp), wkb[0][:, (half * 4 + i) * 128:(half * 4 + i + 1) * 128],
                           xach[i][:, scp * 512:(scp + 1) * 512],
                           r=(wkb[1], XB[i]), w=(PSB[scp],), start=(ii == 0), stop=(ii == 3))
                for scp in range(8):
                    dst = kpair[:, scp * 512:(scp + 1) * 512]
                    bref = kpB[scp // 2]
                    if half == 0:
                        ve_copy(dst, bank(scp), r=(PSB[scp],), w=(bref,))
                    else:
                        ve_tt(dst, dst, bank(scp), OP.add, r=(PSB[scp], bref), w=(bref,))
                if "kph0" in tapd and pair == 0 and half == 0:
                    dma("gp", tapd["kph0"], kpair, r=tuple(kpB), w=())
                for ii, i in enumerate(iord):
                    for scp in range(8):
                        mm(bank(scp), wvb[0][:, (half * 4 + i) * 128:(half * 4 + i + 1) * 128],
                           xach[i][:, scp * 512:(scp + 1) * 512],
                           r=(wvb[1], XB[i]), w=(PSB[scp],), start=(ii == 0), stop=(ii == 3))
                for scp in range(8):
                    dst = tV[:, scp * 512:(scp + 1) * 512]
                    if hidx2 == 0:
                        ve_copy(dst, bank(scp), r=(PSB[scp],) + tuple(VB), w=tuple(VB))
                    else:
                        ve_tt(dst, dst, bank(scp), OP.add, r=(PSB[scp],) + tuple(VB),
                              w=tuple(VB))
                for ii, i in enumerate(iord):
                    for qt in range(4):
                        rhs = xach[i].rearrange("p (b c) -> p b c", c=128)[:, 8 * qt: 8 * qt + 8: 2, :]
                        mm(bank(qt), wqb[0][:, (half * 4 + i) * 128:(half * 4 + i + 1) * 128],
                           rhs, r=(wqb[1], XB[i]), w=(PSB[qt],), start=(ii == 0), stop=(ii == 3))
                for qt in range(4):
                    dst = qpair[:, qt * 512:(qt + 1) * 512]
                    bref = qpB[qt // 2]
                    if half == 0:
                        ve_copy(dst, bank(qt), r=(PSB[qt],), w=(bref,))
                    else:
                        ve_tt(dst, dst, bank(qt), OP.add, r=(PSB[qt], bref), w=(bref,))

            for sc in range(31, -1, -1):
                bk = sc % 2
                tr(bank(bk)[:, 0:128], tV[:, sc * 128:(sc + 1) * 128], cID[:, :],
                   r=tuple(VB) + (CB,), w=(PSB[bk],))
                vt = vtile(sc)
                dst = vt[:, 0:130].rearrange("p (g d) -> p g d", d=65)[:, :, 0:64]
                srcv = bank(bk)[:, 0:128].rearrange("p (g d) -> p g d", d=64)
                ve_copy(dst, srcv, r=(PSB[bk],) + tuple(VB), w=tuple(VB))
            for oc in (64, 129):
                ve_memset(tV[:, 0:4160].rearrange("p (s c) -> p s c", c=130)[:, :, oc:oc + 1],
                          1.0, w=tuple(VB))

            if "probe" in tapd and pair == 0:
                dma("gp", tapd["probe"], sqa[:, 0:512], r=(XB[0],), w=())
            if "kpair" in tapd and pair == 0:
                dma("gp", tapd["kpair"], kpair, r=tuple(kpB), w=())
            if "qpair" in tapd and pair == 0:
                dma("gp", tapd["qpair"], qpair, r=tuple(qpB), w=())
            if "vtiles" in tapd and pair == 0:
                dma("gp", tapd["vtiles"], tV[:, :], r=tuple(VB), w=())

            normq2f = tXA[0:2, 4096:6144]
            srow2 = tXA[0:2, 6144:8192]
            iota2 = tXA[0:2, 8192:10240]
            normk2f = tXA[0:2, 10240:14336]
            slope2 = tXA[0:2, 12288:14336]
            maxk2b = tM[0:2, 130:131]
            maxk8b = tM[0:2, 131:132]
            act(sqa[:, 0:4096], kpair[:, :], AF.Square, r=tuple(kpB), w=(XB[0],))
            for ch in range(8):
                mm(bank(ch)[0:2, :], cON2[:, :], sqa[:, ch * 512:(ch + 1) * 512],
                   r=(CB, XB[0]), w=(PSB[ch],))
            for i4 in range(4):
                ve_copy(normk2f[:, i4 * 1024:(i4 + 1) * 1024], PS[i4][0:2, :],
                        r=(PSB[2 * i4], PSB[2 * i4 + 1]), w=(XB[2], XB[3]))
            ve_red(maxk2b, normk2f, "max", r=(XB[2], XB[3]), w=(MB,))
            act(maxk8b, maxk2b, AF.Sqrt, r=(MB,), w=(MB,), scale=1.0 / 64.0)
            act(sqa[:, 0:2048], qpair[:, :], AF.Square, r=tuple(qpB) + (XB[0],), w=(XB[0],))
            for ch in range(4):
                mm(bank(ch)[0:2, :], cON2[:, :], sqa[:, ch * 512:(ch + 1) * 512],
                   r=(CB, XB[0]), w=(PSB[ch],))
            for i4 in range(2):
                ve_copy(normq2f[:, i4 * 1024:(i4 + 1) * 1024], PS[i4][0:2, :],
                        r=(PSB[2 * i4], PSB[2 * i4 + 1]), w=(XB[1],))
            ve_ts(srow2, normq2f, 1e-30, OP.add, r=(XB[1],), w=(XB[1],))
            act(srow2, srow2, AF.Sqrt, r=(XB[1],), w=(XB[1],))
            for r2 in range(2):
                dma("gp", iota2[r2:r2 + 1, :], iota_in, r=(XB[2],), w=(XB[2],))
            ve_tt(slope2, iota2, cSLP[:, pair:pair + 1].broadcast_to((2, TOWN)),
                  OP.mult, r=(XB[2], XB[3], CB), w=(XB[3],))
            ve_tt(srow2, srow2, maxk8b.broadcast_to((2, TOWN)), OP.mult,
                  r=(XB[1], MB), w=(XB[1],))
            ve_tt(srow2, srow2, slope2, OP.add, r=(XB[1], XB[3]), w=(XB[1],))
            if pair < 7:
                for i in (2, 3):
                    dma("gp", xach[i], xnf[i * 128:(i + 1) * 128, :],
                        r=(XB[i], XNF), w=(XB[i],), stream="gS")

            for qt in range(4):
                for hh in range(2):
                    mm(bank(6), cSL2[:, hh * 128:(hh + 1) * 128],
                       srow2[:, qt * 512:(qt + 1) * 512],
                       r=(CB, XB[1]), w=(PSB[6],))
                    ve_copy(shbc[hh][0], bank(6), r=(PSB[6],), w=(shbc[hh][1],))
                scmax = 8 * qt + 7
                av_bank = (PSB[4], PSB[5])
                pend = None

                def emit_av(sc, js):
                    for hh in range(2):
                        pb, pbB = pbuf[hh][sc % 2]
                        mm(bank(4 + hh)[0:65, js * 128:512],
                           vtile(sc)[:, hh * 65:(hh + 1) * 65],
                           pb[:, js * 128:512],
                           r=(pbB, vtB(sc)), w=(av_bank[hh],),
                           start=(sc == 0), stop=(sc == scmax))

                for sc in range(scmax + 1):
                    js = min(j for j in range(4) if sc <= 2 * (4 * qt + j) + 1)
                    w_ = 512 - js * 128
                    for hh in range(2):
                        h = 2 * pair + hh
                        rows = slice(hh * 64, (hh + 1) * 64)
                        bkid = (sc % 3) if hh == 0 else (3, 6, 7)[sc % 3]
                        sB = PSB[bkid]
                        sap = bank(bkid)
                        mm(sap[:, js * 128: 512], kpair[rows, sc * 128:(sc + 1) * 128],
                           qpair[rows, qt * 512 + js * 128:(qt + 1) * 512],
                           r=tuple(kpB) + tuple(qpB), w=(sB,))
                        ve_stt(sap[:, js * 128:512], sap[:, js * 128:512], 0.125,
                               shbc[hh][0][:, js * 128:512], OP.mult, OP.subtract,
                               r=(sB, shbc[hh][1]), w=(sB,))
                        for j in range(js, 4):
                            d = sc - 2 * (4 * qt + j)
                            if d == 0:
                                mk = cMA
                            elif d == 1:
                                mk = cMB
                            else:
                                continue
                            ve_tt(sap[:, j * 128:(j + 1) * 128], sap[:, j * 128:(j + 1) * 128],
                                  mk[:, :], OP.subtract, r=(sB, CB), w=(sB,))
                        pb, pbB = pbuf[hh][sc % 2]
                        act(pb[:, js * 128:512], sap[:, js * 128:512], AF.Exp,
                            r=(sB, CB), w=(pbB,), bias=cALI[:, h * 32 + sc: h * 32 + sc + 1])
                    if pend is not None:
                        emit_av(*pend)
                    pend = (sc, js)
                emit_av(*pend)

                ocols = slice(pair * 2048 + qt * 512, pair * 2048 + (qt + 1) * 512)
                dn2 = tXA[0:1, 0:4096]
                for hh in range(2):
                    ve_copy(dn2[0:1, hh * 2048 + qt * 512: hh * 2048 + (qt + 1) * 512],
                            bank(4 + hh)[64:65, :], r=(av_bank[hh],), w=(XB[0],))
                    if hh == 0:
                        ve_copy(tO[0:64, ocols], bank(4)[0:64, :],
                                r=(av_bank[0],), w=(OB[pair],))
                    else:
                        stg = tP[0:64, 0:512]
                        ve_copy(stg, bank(5)[0:64, :], r=(av_bank[1],), w=(PB[0],))
                        dma("gp", tO[64:128, ocols], stg, r=(PB[0],), w=(OB[pair],))

            dn2 = tXA[0:1, 0:4096]
            P.emit("ve", lambda e, a=dn2: e.reciprocal(a, a), r=(XB[0],), w=(XB[0],))
            for q4 in range(4):
                for hh in range(2):
                    oc = slice(pair * 2048 + q4 * 512, pair * 2048 + (q4 + 1) * 512)
                    bk = 6 + hh
                    mm(bank(bk), cBCO[0:1, :],
                       dn2[0:1, hh * 2048 + q4 * 512: hh * 2048 + (q4 + 1) * 512],
                       r=(CB, XB[0]), w=(PSB[bk],))
                    if hh == 0:
                        ve_tt(tO[0:64, oc], tO[0:64, oc], bank(6)[0:64, :], OP.mult,
                              r=(OB[pair], PSB[6]), w=(OB[pair],))
                    else:
                        ve_tt(tO[64:128, oc], tO[64:128, oc], bank(7)[64:128, :], OP.mult,
                              r=(OB[pair], PSB[7]), w=(OB[pair],))
        if "oall" in tapd:
            for cc in range(8):
                dma("gp", tapd["oall"][cc * 128:(cc + 1) * 128, :],
                    tO[:, cc * 2048:(cc + 1) * 2048], r=(OB[cc],), w=())
        P.barrier()

        acc = [tXA[:, dc * 2048:(dc + 1) * 2048] for dc in range(8)]
        accB = [XB[dc // 2] for dc in range(8)]
        h1s = [(tO[:, ic * 1024:(ic + 1) * 1024], OB[ic // 2]) for ic in range(8)]
        h3s = [(tO[:, 8192 + ic * 1024: 8192 + (ic + 1) * 1024], OB[4 + ic // 2]) for ic in range(8)]
        wfb = [(tP[:, 0:1024], (PB[0], PB[1])), (tW[:, 0:1024], (WB[0],))]
        gwrk = tP[:, 1024:2048]
        gwB = (PB[2], PB[3])
        tmpz = (tQ[:, 1024:1536], QB[1])
        cB2 = tW[0:8, 2048:3072]
        dma("sy", cB2, b2, r=(WB[2],), w=(WB[2],))
        combT2 = [(tSH[0:8, 0:1024], SHB[0]), (tW[0:8, 1024:2048], WB[1])]
        logi = (tQ[0:8, 0:1024], QB[0])
        cums = (tQ[0:8, 1024:2048], QB[1])
        h1sp = [tO[:, ic * CCAP:(ic + 1) * CCAP] for ic in range(8)]
        h3sp = [tO[:, 3072 + ic * CCAP: 3072 + (ic + 1) * CCAP] for ic in range(8)]
        xgt = [tO[:, 6144 + i * 1024: 6144 + (i + 1) * 1024] for i in range(3)]
        xgf = [tO[:, 9216 + cc * CCAP: 9216 + (cc + 1) * CCAP] for cc in range(8)]
        ztok = [tO[:, 12288 + i * 1024: 12288 + (i + 1) * 1024] for i in range(3)]

        ltok = tP[:, 1024:1088]
        pshift = tP[:, 1088:1152]
        mask_all = tP[:, 1152:1216]
        pos_all = tP[:, 1216:1280]
        offu64 = tP[:, 1280:1344]
        p8n_t = tP[:, 1344:1408]
        p8_t = tP[:, 1408:1472]
        cmb_all = tP[:, 1472:1536]
        mg8 = tP[:, 1536:1544]
        sg8 = tP[:, 1544:1552]
        m8b = tP[:, 1552:1560]
        viof8 = tP[:, 1560:1568]
        viou8 = tP[:, 1568:1576]
        basetile = tP[:, 1576:1584]
        counts_sb = tM[0:8, 168:176]
        countsT = tM[0:8, 176:184]
        offs_sb = tM[0:8, 184:192]

        def seg(ap, e=8):
            return ap.rearrange("p (g e) -> p g e", e=e)

        def segb(ap8):
            return ap8.rearrange("p (g o) -> p g o", o=1).broadcast_to((128, 8, 8))

        def emit_routing(th):
            combT = combT2[th]
            for cc in range(8):
                xb_ = wfb[cc % 2]
                src2 = xnf[cc * 128:(cc + 1) * 128, :].rearrange("p (b c) -> p b c", c=128)
                src2 = src2[:, 16 * th: 16 * (th + 1): 2, :]
                dma("gp", xb_[0].rearrange("p (b c) -> p b c", c=128), src2,
                    r=xb_[1] + (XNF,), w=xb_[1])
                for tc in range(2):
                    mm(bank(2 + tc)[0:8, :], cGW[:, cc * 8:(cc + 1) * 8],
                       xb_[0][:, tc * 512:(tc + 1) * 512],
                       r=(CB,) + xb_[1], w=(PSB[2 + tc],), start=(cc == 0), stop=(cc == 7))
            for tc in range(2):
                ve_copy(logi[0][:, tc * 512:(tc + 1) * 512], bank(2 + tc)[0:8, :],
                        r=(PSB[2 + tc],), w=(logi[1],))
            for g in range(8):
                tr(bank(7)[:, g * 8:(g + 1) * 8], logi[0][:, g * 128:(g + 1) * 128],
                   cID[0:8, 0:8], r=(logi[1], CB), w=(PSB[7],), start=(g == 0))
            ve_copy(ltok, bank(7)[:, 0:64], r=(PSB[7],), w=gwB)
            P.emit("ve", lambda e: e.tensor_reduce(seg(mg8, 1), seg(ltok), axis=AX.X,
                                                   op=OP.max, negate=True),
                   r=gwB, w=gwB)
            ve_tt(seg(pshift), seg(ltok), segb(mg8), OP.add, r=gwB, w=gwB)
            act(p8_t, pshift, AF.Exp, r=gwB, w=gwB)
            P.emit("ve", lambda e: e.tensor_reduce(seg(sg8, 1), seg(p8_t), axis=AX.X,
                                                   op=OP.add), r=gwB, w=gwB)
            ve_recip(sg8, sg8, r=gwB, w=gwB)
            ve_tt(seg(p8n_t), seg(p8_t), segb(sg8), OP.mult, r=gwB, w=gwB)
            P.emit("ve", lambda e: e.tensor_reduce(seg(m8b, 1), seg(p8n_t), axis=AX.X,
                                                   op=OP.max), r=gwB, w=gwB)
            ve_tt(seg(pshift), seg(p8n_t), segb(m8b), OP.is_ge, r=gwB, w=gwB)
            ve_stt(p8_t, pshift, -BIG, p8n_t, OP.mult, OP.add, r=gwB, w=gwB)
            P.emit("ve", lambda e: e.tensor_reduce(seg(m8b, 1), seg(p8_t), axis=AX.X,
                                                   op=OP.max), r=gwB, w=gwB)
            ve_tt(seg(p8_t), seg(p8_t), segb(m8b), OP.is_ge, r=gwB, w=gwB)
            ve_tt(ltok, pshift, p8_t, OP.add, r=gwB, w=gwB)
            ve_tt(cmb_all, ltok, p8n_t, OP.mult, r=gwB, w=gwB)
            dma("gp", combd[th * 1024:(th + 1) * 1024, :].rearrange("(g p) e -> p g e", p=128),
                seg(cmb_all), r=gwB, w=(CMD,), stream="gI")
            for g in range(8):
                tr(bank(6 if g < 4 else 5)[0:8, (g % 4) * 128:(g % 4 + 1) * 128],
                   cmb_all[:, g * 8:(g + 1) * 8], cID[:, :],
                   r=gwB + (CB,), w=(PSB[6 if g < 4 else 5],), start=(g % 4 == 0))
            ve_copy(combT[0][:, 0:512], bank(6)[0:8, :], r=(PSB[6],), w=(combT[1],))
            ve_copy(combT[0][:, 512:1024], bank(5)[0:8, :], r=(PSB[5],), w=(combT[1],))
            ve_ts(mask_all, cmb_all, 0.0, OP.is_gt, r=gwB, w=gwB)
            for g in range(8):
                bk = 7 if g < 4 else 4
                mm(bank(bk)[0:8, (g % 4) * 128:(g % 4 + 1) * 128],
                   mask_all[:, g * 8:(g + 1) * 8], cTRI[:, 0:128],
                   r=gwB + (CB,), w=(PSB[bk],), start=(g % 4 == 0))
            ve_copy(cums[0][:, 0:512], bank(7)[0:8, :], r=(PSB[7],), w=(cums[1],))
            ve_copy(cums[0][:, 512:1024], bank(4)[0:8, :], r=(PSB[4],), w=(cums[1],))
            ve_copy(counts_sb,
                    cums[0].rearrange("e (g t) -> e g t", t=128)[:, :, 127],
                    r=(cums[1],), w=(MB,))
            tr(bank(7)[0:8, 0:8], counts_sb, cID[0:8, 0:8], r=(MB, CB), w=(PSB[7],))
            ve_copy(countsT, bank(7)[0:8, 0:8], r=(PSB[7],), w=(MB,))
            mm(bank(7)[0:8, 0:8], countsT, cTRI[0:8, 128:136], r=(MB, CB), w=(PSB[7],))
            ve_copy(offs_sb, bank(7)[0:8, 0:8], r=(PSB[7],), w=(MB,))
            for e_ in range(NEXP):
                lb = (th * NEXP + e_) * 640
                dma("sy", listsd[lb:lb + 640, 0].rearrange("(p c) -> p c", c=5),
                    cSENT[:, :], r=(CB,), w=(LB[e_], PFH[th][e_]), stream="sC",
                    war_only=True)
                ve_memset(basetile[:, e_:e_ + 1], float(lb), w=gwB)
            P.emit("ve", lambda e: e.tensor_tensor(
                cums[0].rearrange("e (g t) -> e g t", t=128),
                cums[0].rearrange("e (g t) -> e g t", t=128),
                offs_sb.rearrange("e (g o) -> e g o", o=1).broadcast_to((8, 8, 128)),
                OP.add), r=(cums[1], MB), w=(cums[1],))
            for g in range(8):
                bk = 7 if g < 4 else 4
                tr(bank(bk)[:, (g % 4) * 8:(g % 4 + 1) * 8],
                   cums[0][:, g * 128:(g + 1) * 128], cID[0:8, 0:8],
                   r=(cums[1], CB), w=(PSB[bk],), start=(g % 4 == 0))
            ve_copy(pos_all[:, 0:32], bank(7)[:, 0:32], r=(PSB[7],), w=gwB)
            ve_copy(pos_all[:, 32:64], bank(4)[:, 0:32], r=(PSB[4],), w=gwB)
            ve_ts(ltok, pos_all, -1.0, OP.add, r=gwB, w=gwB, s2=float(CCAP), op1=OP.min)
            ve_tt(pshift, ltok, padf.broadcast_to((128, 64)), OP.subtract,
                  r=gwB + (MB,), w=gwB)
            ve_tt(pshift, pshift, mask_all, OP.mult, r=gwB, w=gwB)
            ve_tt(pshift, pshift, padf.broadcast_to((128, 64)), OP.add,
                  r=gwB + (MB,), w=gwB)
            P.emit("ve", lambda e: e.tensor_tensor(
                seg(pshift), seg(pshift),
                basetile.rearrange("p (o e) -> p o e", o=1).broadcast_to((128, 8, 8)),
                OP.add), r=gwB, w=gwB)
            ve_copy(offu64.bitcast(mybir.dt.uint32), pshift, r=gwB, w=gwB)
            for g in range(8):
                ve_ts(viof8[:, g:g + 1], cI128[:, :], float(th * 1024 + g * 128), OP.add,
                      r=(CB,), w=gwB)
            ve_copy(viou8.bitcast(mybir.dt.uint32), viof8, r=gwB, w=gwB)
            for g in range(8):
                for e_ in range(NEXP):
                    idma_scatter(listsd, viou8.bitcast(mybir.dt.uint32)[:, g:g + 1],
                                 offu64.bitcast(mybir.dt.uint32)[:, g * 8 + e_: g * 8 + e_ + 1],
                                 r=gwB + (PFH[th][e_],), w=(LB[e_],), war_only=True)
            if "comb" in tapd:
                dma("gp", tapd["comb"][:, th * 1024:(th + 1) * 1024], combT[0],
                    r=(combT[1],), w=())

        wo_tiles = []
        for cc in range(8):
            if cc < 4:
                ap, bref = tK[:, cc * 1024:(cc + 1) * 1024], KB[cc]
            else:
                ap, bref = tV[:, (cc - 4) * 1024:(cc - 3) * 1024], VB[cc - 4]
            wo_tiles.append((ap, bref))
            dma("sy", ap, woT[cc * 128:(cc + 1) * 128, :], r=(bref,), w=(bref,))

        def emit_oproj(dcs):
            for dc in dcs:
                for tc in range(4):
                    pbk = PSB[tc % 2]
                    for cc in range(8):
                        mm(bank(tc % 2), wo_tiles[cc][0][:, dc * 128:(dc + 1) * 128],
                           tO[:, cc * 2048 + tc * 512: cc * 2048 + (tc + 1) * 512],
                           r=(wo_tiles[cc][1], OB[cc]), w=(pbk,),
                           start=(cc == 0), stop=(cc == 7))
                    ve_copy(acc[dc][:, tc * 512:(tc + 1) * 512], bank(tc % 2),
                            r=(pbk,), w=(accB[dc],))

        emit_routing(0)
        emit_oproj(range(0, 3))
        emit_routing(1)
        emit_oproj(range(3, 8))
        for th_ in range(2):
            for dc in range(8):
                for tc in range(2):
                    mm(bank(2 + tc), cB2[0:8, dc * 128:(dc + 1) * 128],
                       combT2[th_][0][:, tc * 512:(tc + 1) * 512],
                       r=(WB[2], combT2[th_][1]), w=(PSB[2 + tc],))
                    aslc = acc[dc][:, th_ * 1024 + tc * 512: th_ * 1024 + (tc + 1) * 512]
                    ve_tt(aslc, aslc, bank(2 + tc), OP.add,
                          r=(accB[dc], PSB[2 + tc]), w=(accB[dc],))
        if "accp3" in tapd:
            for dc in range(8):
                dma("gp", tapd["accp3"][dc * 128:(dc + 1) * 128, :], acc[dc],
                    r=(accB[dc],), w=())
        P.barrier()

        xnh = []
        for cc in range(8):
            if cc < 4:
                xnh.append((tK[:, cc * 1024:(cc + 1) * 1024], KB[cc]))
            else:
                xnh.append((tV[:, (cc - 4) * 1024:(cc - 3) * 1024], VB[cc - 4]))

        for th in range(2):
            U32 = mybir.dt.uint32
            lcol2 = [tP[:, 1592:1595], tP[:, 1600:1603]]
            gcol2 = [tP[:, 1596:1599], tP[:, 1604:1607]]
            cgt2 = [[tO[:, 15360 + i * 8: 15368 + i * 8] for i in range(3)],
                    [tO[:, 15384 + i * 8: 15392 + i * 8] for i in range(3)]]

            def emit_route_load(e):
                sl = e % 2
                lb = (th * NEXP + e) * 640
                dma("gp", lcol2[sl].bitcast(U32),
                    listsd[lb:lb + CCAP, 0].rearrange("(p c) -> p c", c=3),
                    r=(LB[e], PB[3]), w=(PB[3],), stream="gI")
                ve_ts(gcol2[sl].bitcast(U32), lcol2[sl].bitcast(U32),
                      float(TOWN - 1), OP.min, r=(PB[3],), w=(PB[3],))
                for i in range(3):
                    idma_gather(xgt[i], xntd, gcol2[sl].bitcast(U32)[:, i:i + 1],
                                r=(XNT, PB[3]) + ob(6144 + i * 1024, 6144 + (i + 1) * 1024),
                                w=ob(6144 + i * 1024, 6144 + (i + 1) * 1024))
                    idma_gather(cgt2[sl][i], combd, lcol2[sl].bitcast(U32)[:, i:i + 1],
                                r=(CMD, PB[3]) + ob(15360, 15408),
                                w=ob(15360 + sl * 24 + i * 8, 15368 + sl * 24 + i * 8))

            def emit_xg_tr(e):
                for i in range(3):
                    for cc in range(8):
                        bk = 4 + (i * 8 + cc) % 2
                        tr(bank(bk)[:, 0:128], xgt[i][:, cc * 128:(cc + 1) * 128], cID[:, :],
                           r=ob(6144 + i * 1024, 6144 + (i + 1) * 1024) + (CB,), w=(PSB[bk],))
                        ve_copy(xgf[cc][:, i * 128:(i + 1) * 128], bank(bk)[:, 0:128],
                                r=(PSB[bk],),
                                w=ob(9216 + cc * CCAP, 9216 + (cc + 1) * CCAP))

            emit_route_load(0)
            for cc in range(8):
                src = xnf[cc * 128:(cc + 1) * 128, :].rearrange("p (b c) -> p b c", c=128)
                src = src[:, 16 * th: 16 * (th + 1): 2, :]
                dma("gp", xnh[cc][0].rearrange("p (b c) -> p b c", c=128), src,
                    r=(xnh[cc][1], XNF), w=(xnh[cc][1],))
            emit_xg_tr(0)
            for e in range(NEXP):
                sl = e % 2
                for mat, hdst, is1 in ((w1[e], h1sp, True), (w3[e], h3sp, False)):
                    for ic in range(8):
                        wb_ = wfb[ic % 2]
                        dma("sy", wb_[0].rearrange("p (c d) -> p c d", d=128),
                            mat[:, ic * 128:(ic + 1) * 128].rearrange("(c p) d -> p c d", p=128),
                            r=wb_[1], w=wb_[1])
                        pb2 = (2 if not is1 else 0) + (ic % 2)
                        for cc in range(8):
                            mm(bank(pb2)[:, 0:CCAP], wb_[0][:, cc * 128:(cc + 1) * 128],
                               xgf[cc],
                               r=wb_[1] + ob(9216 + cc * CCAP, 9216 + (cc + 1) * CCAP),
                               w=(PSB[pb2],), start=(cc == 0), stop=(cc == 7))
                        bias_ap = (cB1 if is1 else cB3)[:, e * 8 + ic: e * 8 + ic + 1]
                        base = 0 if is1 else 3072
                        act(hdst[ic], bank(pb2)[:, 0:CCAP],
                            AF.Silu if is1 else AF.Identity,
                            r=(PSB[pb2], CB),
                            w=ob(base + ic * CCAP, base + (ic + 1) * CCAP), bias=bias_ap)
                if e + 1 < NEXP:
                    emit_route_load(e + 1)
                for ic in range(8):
                    ve_tt(h1sp[ic], h1sp[ic], h3sp[ic], OP.mult,
                          r=ob(ic * CCAP, (ic + 1) * CCAP) + ob(3072 + ic * CCAP, 3072 + (ic + 1) * CCAP),
                          w=ob(ic * CCAP, (ic + 1) * CCAP))
                for dc in range(8):
                    wb_ = wfb[dc % 2]
                    dma("sy", wb_[0].rearrange("p (c d) -> p c d", d=128),
                        w2[e][:, dc * 128:(dc + 1) * 128].rearrange("(c p) d -> p c d", p=128),
                        r=wb_[1], w=wb_[1])
                    pb2 = dc % 2
                    for ic in range(8):
                        mm(bank(pb2)[:, 0:CCAP], wb_[0][:, ic * 128:(ic + 1) * 128],
                           h1sp[ic], r=wb_[1] + ob(ic * CCAP, (ic + 1) * CCAP),
                           w=(PSB[pb2],), start=(ic == 0), stop=(ic == 7))
                    ve_copy(h3sp[dc], bank(pb2)[:, 0:CCAP], r=(PSB[pb2],),
                            w=ob(3072 + dc * CCAP, 3072 + (dc + 1) * CCAP))
                if e + 1 < NEXP:
                    emit_xg_tr(e + 1)
                for i in range(3):
                    for dc in range(8):
                        bk = 6 + dc % 2
                        tr(bank(bk)[:, 0:128], h3sp[dc][:, i * 128:(i + 1) * 128],
                           cID[:, :],
                           r=ob(3072 + dc * CCAP, 3072 + (dc + 1) * CCAP) + (CB,),
                           w=(PSB[bk],), start=True)
                        ve_copy(ztok[i][:, dc * 128:(dc + 1) * 128], bank(bk)[:, 0:128],
                                r=(PSB[bk],),
                                w=ob(12288 + i * 1024, 12288 + (i + 1) * 1024))
                    ve_tt(ztok[i], ztok[i],
                          cgt2[sl][i][:, e:e + 1].broadcast_to((128, 1024)),
                          OP.mult,
                          r=ob(12288 + i * 1024, 12288 + (i + 1) * 1024)
                            + ob(15360 + sl * 24 + i * 8, 15368 + sl * 24 + i * 8),
                          w=ob(12288 + i * 1024, 12288 + (i + 1) * 1024))
                    idma_scatter(yffn, ztok[i], lcol2[sl].bitcast(U32)[:, i:i + 1],
                                 r=ob(12288 + i * 1024, 12288 + (i + 1) * 1024) + (PB[3], YFB),
                                 w=(YFB,), accum=True)

            for e in (NEXP,):
                wsrc1, wsrc3, wsrc2 = sw1, sw3, sw2
                for mat, hdst, fn_, bias_ in ((wsrc1, h1s, AF.Silu, cSB1),
                                              (wsrc3, h3s, AF.Copy, cSB3)):
                    for ic in range(8):
                        wb_ = wfb[ic % 2]
                        dma("sy", wb_[0].rearrange("p (c d) -> p c d", d=128),
                            mat[:, ic * 128:(ic + 1) * 128].rearrange("(c p) d -> p c d", p=128),
                            r=wb_[1], w=wb_[1])
                        pb2 = (4 if mat is wsrc3 else 0) + (ic % 2) * 2
                        for cc in range(8):
                            for tc in range(2):
                                mm(bank(pb2 + tc), wb_[0][:, cc * 128:(cc + 1) * 128],
                                   xnh[cc][0][:, tc * 512:(tc + 1) * 512],
                                   r=wb_[1] + (xnh[cc][1],), w=(PSB[pb2 + tc],),
                                   start=(cc == 0), stop=(cc == 7))
                        bias_ap = bias_[:, ic:ic + 1]
                        for tc in range(2):
                            act(h1s[ic][0][:, tc * 512:(tc + 1) * 512] if fn_ == AF.Silu
                                else h3s[ic][0][:, tc * 512:(tc + 1) * 512],
                                bank(pb2 + tc),
                                AF.Silu if fn_ == AF.Silu else AF.Identity,
                                r=(PSB[pb2 + tc], CB),
                                w=((h1s[ic][1],) if fn_ == AF.Silu else (h3s[ic][1],)),
                                bias=bias_ap)
                for ic in range(8):
                    ve_tt(h1s[ic][0], h1s[ic][0], h3s[ic][0], OP.mult,
                          r=(h1s[ic][1], h3s[ic][1]), w=(h1s[ic][1],))
                for dc in range(8):
                    wb_ = wfb[dc % 2]
                    dma("sy", wb_[0].rearrange("p (c d) -> p c d", d=128),
                        wsrc2[:, dc * 128:(dc + 1) * 128].rearrange("(c p) d -> p c d", p=128),
                        r=wb_[1], w=wb_[1])
                    pb2 = (dc % 2) * 2
                    for ic in range(8):
                        for tc in range(2):
                            mm(bank(pb2 + tc), wb_[0][:, ic * 128:(ic + 1) * 128],
                               h1s[ic][0][:, tc * 512:(tc + 1) * 512],
                               r=wb_[1] + (h1s[ic][1],), w=(PSB[pb2 + tc],),
                               start=(ic == 0), stop=(ic == 7))
                    for tc in range(2):
                        aslc = acc[dc][:, th * 1024 + tc * 512: th * 1024 + (tc + 1) * 512]
                        act(tmpz[0], bank(pb2 + tc), AF.Identity,
                            r=(PSB[pb2 + tc], CB), w=(tmpz[1],), bias=cSB2[:, dc:dc + 1])
                        ve_tt(aslc, aslc, tmpz[0], OP.add,
                              r=(accB[dc], tmpz[1]), w=(accB[dc],))
        P.barrier()

        outrow = [(tO[:, 0:1024], OB[0]), (tO[:, 1024:2048], OB[4])]
        xyrow = [(tO[:, 2048:3072], OB[1]), (tO[:, 3072:4096], OB[5])]
        xrow2 = [(tO[:, 4096:5120], OB[2]), (tO[:, 5120:6144], OB[6])]
        yfr2 = [(tO[:, 6144:7168], OB[3]), (tO[:, 7168:8192], OB[7])]
        for jp in range(NOWN // 2):
            for j in (2 * jp, 2 * jp + 1):
                sl = j % 2
                xr, xrB = xrow2[sl]
                yr, yrB = yfr2[sl]
                dma("gp", xr, x[2 * j * 128:(2 * j + 1) * 128, :], r=(xrB,), w=(xrB,))
                dma("gp", yr, yffn[j * 128:(j + 1) * 128, :], r=(YFB, yrB), w=(yrB,))
            for j in (2 * jp, 2 * jp + 1):
                sl = j % 2
                xy, xyB = xyrow[sl]
                xr, xrB = xrow2[sl]
                yr, yrB = yfr2[sl]
                ps = PS[j % 2]
                psb = (PSB[2 * (j % 2)], PSB[2 * (j % 2) + 1])
                ve_tt(xy, xr, yr, OP.add, r=(xrB, yrB, xyB), w=(xyB,))
                for dc in range(8):
                    tr(ps[:, dc * 128:(dc + 1) * 128], acc[dc][:, j * 128:(j + 1) * 128],
                       cID[:, :], r=(accB[dc], CB), w=(psb[dc // 4],), start=(dc % 4 == 0))
            for j in (2 * jp, 2 * jp + 1):
                sl = j % 2
                orow, orB = outrow[sl]
                xy, xyB = xyrow[sl]
                ps = PS[j % 2]
                psb = (PSB[2 * (j % 2)], PSB[2 * (j % 2) + 1])
                ve_tt(orow, xy, ps[:, :1024], OP.add, r=psb + (xyB, orB), w=(orB,))
                dma("gp", y[j * 128:(j + 1) * 128, :], orow, r=(orB,), w=(orB,))
        P.barrier()

        block = es.enter_context(nc.Block())
        eng_attr = {"te": "tensor", "ve": "vector", "se": "scalar",
                    "sy": "sync", "gp": "gpsimd"}
        for ename in ENGS:
            opl = P.ops[ename]
            if not opl:
                continue

            def body(eng, _opl=opl, _en=ename):
                for waits, fn, step, tag in _opl:
                    for dsem, v in waits:
                        eng.wait_ge(sem[dsem], v)
                    if fn is not None:
                        ins = fn(eng)
                        if step:
                            ins.then_inc(sem[tag], step)
            getattr(block, eng_attr[ename])(body)
    return nc


def _host_inputs(inp):
    x = np.asarray(inp["L0_attn_x"], dtype=np.float32)
    com = {
        "wqT": np.ascontiguousarray(np.asarray(inp["L0_attn_Wq"], np.float32).T),
        "wkT": np.ascontiguousarray(np.asarray(inp["L0_attn_Wk"], np.float32).T),
        "wvT": np.ascontiguousarray(np.asarray(inp["L0_attn_Wv"], np.float32).T),
        "woT": np.ascontiguousarray(np.asarray(inp["L0_attn_Wo"], np.float32).T),
        "gwT": np.ascontiguousarray(np.asarray(inp["L0_ffn_gate_w"], np.float32).T),
        "w1": np.ascontiguousarray(np.asarray(inp["L0_ffn_w1"], np.float32)),
        "w2": np.ascontiguousarray(np.asarray(inp["L0_ffn_w2"], np.float32)),
        "w3": np.ascontiguousarray(np.asarray(inp["L0_ffn_w3"], np.float32)),
        "b2": np.ascontiguousarray(np.asarray(inp["L0_ffn_b2"], np.float32)),
        "sw1": np.ascontiguousarray(np.asarray(inp["L0_ffn_sw1"], np.float32)),
        "sw2": np.ascontiguousarray(np.asarray(inp["L0_ffn_sw2"], np.float32)),
        "sw3": np.ascontiguousarray(np.asarray(inp["L0_ffn_sw3"], np.float32)),
    }
    for nm, key in (("b1c", "L0_ffn_b1"), ("b3c", "L0_ffn_b3")):
        bmat = np.asarray(inp[key], np.float32)
        com[nm] = np.ascontiguousarray(
            bmat.reshape(8, 8, 128).transpose(2, 0, 1).reshape(128, 64))
    for nm, key in (("sb1c", "L0_ffn_sb1"), ("sb2c", "L0_ffn_sb2"), ("sb3c", "L0_ffn_sb3")):
        v = np.asarray(inp[key], np.float32)
        com[nm] = np.ascontiguousarray(v.reshape(8, 128).T)
    com["ident"] = np.eye(128, dtype=np.float32)
    ctri = np.zeros((128, 144), dtype=np.float32)
    ctri[:, 0:128] = (np.arange(128)[:, None] <= np.arange(128)[None, :])
    ctri[0:16, 128:144] = (np.arange(16)[:, None] < np.arange(16)[None, :])
    com["ctri"] = ctri
    com["ci128"] = np.arange(128, dtype=np.float32).reshape(128, 1)
    con2 = np.zeros((128, 2), dtype=np.float32)
    con2[0:64, 0] = 1.0
    con2[64:128, 1] = 1.0
    com["cones2"] = con2
    csl2 = np.zeros((2, 256), dtype=np.float32)
    csl2[0, 0:128] = 1.0
    csl2[1, 128:256] = 1.0
    com["csel2"] = csl2
    cslp = np.zeros((2, NEXP), dtype=np.float32)
    for p_ in range(NEXP):
        cslp[0, p_] = SLOPES[2 * p_]
        cslp[1, p_] = SLOPES[2 * p_ + 1]
    com["cslopes"] = cslp
    com["bco"] = np.ones((1, 128), dtype=np.float32)
    com["onescol"] = np.ones((128, 1), dtype=np.float32)
    tri = np.where(np.arange(128)[:, None] > np.arange(128)[None, :], BIG, 0.0).astype(np.float32)
    in_maps = []
    for core in range(8):
        b, par = core // 2, core % 2
        m = dict(com)
        xb = x[b].reshape(NBLK, 128, C)
        if par == 1:
            perm = np.arange(NBLK).reshape(NBLK // 2, 2)[:, ::-1].reshape(NBLK)
            xb = xb[perm]
        m["x"] = np.ascontiguousarray(xb.reshape(T, C))
        tb = np.arange(NBLK) ^ par
        ali = np.zeros((128, 512), dtype=np.float32)
        for h in range(H):
            for sc in range(NBLK):
                ali[:, h * 32 + sc] = SLOPES[h] * (tb[sc] * 128 + np.arange(128))
        m["alibi"] = ali
        m["maska"] = tri
        m["maskb"] = (np.full((128, 128), BIG, dtype=np.float32) if par == 0
                      else np.zeros((128, 128), dtype=np.float32))
        iota = np.zeros((1, TOWN), dtype=np.float32)
        for j in range(NOWN):
            iota[0, j * 128:(j + 1) * 128] = (2 * j + par) * 128 + np.arange(128)
        m["iota"] = iota
        in_maps.append(m)
    return in_maps


def kernel(**inp):
    in_maps = _host_inputs(inp)
    nc = _build()
    res = run_bass_kernel_spmd(nc, in_maps, core_ids=list(range(8)))
    out = np.zeros((B, T, C), dtype=np.float32)
    for core in range(8):
        b, par = core // 2, core % 2
        yc = np.asarray(res.results[core]["y"]).reshape(NOWN, 128, C)
        for j in range(NOWN):
            out[b, (2 * j + par) * 128:(2 * j + par + 1) * 128, :] = yc[j]
    return out


# version 35
# speedup vs baseline: 1.0648x; 1.0648x over previous
import contextlib
import math

import numpy as np

import concourse.bass as bass
import concourse.mybir as mybir
from concourse.bass_utils import run_bass_kernel_spmd
from concourse.alu_op_type import AluOpType as OP

F32 = mybir.dt.float32
AF = mybir.ActivationFunctionType
AX = mybir.AxisListType

B, T, C = 4, 4096, 1024
H, DH = 16, 64
NBLK = 32
NOWN = 16
TOWN = NOWN * 128
NCC = 8
NEXP = 8
CCAP = 384
SLOPES = [2.0 ** (-(h + 1) / 2.0) for h in range(H)]
BIG = 1e30

ENGS = ("te", "ve", "se", "sy", "gp")
STREAMS = {"sW": "sy", "sC": "sy", "gX": "gp", "gI": "gp", "gS": "gp"}
SEMS = ENGS + tuple(STREAMS)


class _Buf:
    __slots__ = ("w", "readers")

    def __init__(self):
        self.w = None
        self.readers = {}


class _Prog:
    def __init__(self):
        self.ops = {e: [] for e in ENGS}
        self.cnt = {s: 0 for s in SEMS}
        self.waited = {e: {} for e in ENGS}

    SELF_MARGIN = 6

    def emit(self, eng, fn, r=(), w=(), step=1, stream=None, war_only=False):
        tag = stream or eng
        waits = {}

        def need(dep, val):
            if dep == eng and dep in ("te", "ve", "se"):
                if self.cnt[eng] - val >= self.SELF_MARGIN:
                    return
            if dep in STREAMS:
                val = self.cnt[dep]
            if self.waited[eng].get(dep, 0) >= val:
                return
            if waits.get(dep, 0) >= val:
                return
            assert val <= self.cnt[dep], (eng, dep, val, self.cnt[dep])
            waits[dep] = val

        for buf in r:
            if buf.w is not None:
                need(*buf.w)
        for buf in w:
            if buf.w is not None and not war_only:
                need(*buf.w)
            for de, dv in buf.readers.items():
                need(de, dv)
        self.cnt[tag] += step
        after = self.cnt[tag]
        for de, dv in waits.items():
            self.waited[eng][de] = max(self.waited[eng].get(de, 0), dv)
        self.ops[eng].append((sorted(waits.items()), fn, step, tag))
        for buf in r:
            buf.readers[tag] = after
        for buf in w:
            buf.w = (tag, after)
            buf.readers = {}

    def barrier(self):
        snap = dict(self.cnt)
        for eng in ENGS:
            waits = []
            for de in SEMS:
                if de == eng and de in ("te", "ve", "se"):
                    continue
                v = snap[de]
                if v and self.waited[eng].get(de, 0) < v:
                    waits.append((de, v))
                    self.waited[eng][de] = v
            if waits:
                self.ops[eng].append((waits, None, 0, eng))


def _build(taps=()):
    nc = bass.Bass("TRN2", target_bir_lowering=False)

    def din(name, shape):
        return nc.dram_tensor(name, shape, F32, kind="ExternalInput").ap()

    x = din("x", [T, C])
    wqT = din("wqT", [C, C])
    wkT = din("wkT", [C, C])
    wvT = din("wvT", [C, C])
    woT = din("woT", [C, C])
    gwT = din("gwT", [C, NEXP])
    w1 = din("w1", [NEXP, C, C])
    w2 = din("w2", [NEXP, C, C])
    w3 = din("w3", [NEXP, C, C])
    b2 = din("b2", [NEXP, C])
    b1c = din("b1c", [128, 64])
    b3c = din("b3c", [128, 64])
    sw1 = din("sw1", [C, C])
    sw2 = din("sw2", [C, C])
    sw3 = din("sw3", [C, C])
    sb1c = din("sb1c", [128, 8])
    sb2c = din("sb2c", [128, 8])
    sb3c = din("sb3c", [128, 8])
    ident = din("ident", [128, 128])
    bco = din("bco", [1, 128])
    onescol = din("onescol", [128, 1])
    alibi_in = din("alibi", [128, 512])
    maska_in = din("maska", [128, 128])
    maskb_in = din("maskb", [128, 128])
    iota_in = din("iota", [1, TOWN])
    ctri_in = din("ctri", [128, 144])
    ci128_in = din("ci128", [128, 1])
    cones2_in = din("cones2", [128, 2])
    cslopes_in = din("cslopes", [2, NEXP])
    csel2_in = din("csel2", [2, 256])

    y = nc.dram_tensor("y", [TOWN, C], F32, kind="ExternalOutput").ap()
    xnf = nc.dram_tensor("xnf", [C, T], F32,
                         kind="ExternalOutput" if "xnf" in taps else "Internal").ap()
    xntd = nc.dram_tensor("xnt", [TOWN, C], F32).ap()
    combd = nc.dram_tensor("combd", [TOWN + 512, NEXP], F32,
                           kind="ExternalOutput" if "combd" in taps else "Internal").ap()
    listsd = nc.dram_tensor("listsd", [2 * NEXP * 640, 1], mybir.dt.uint32,
                            kind="ExternalOutput" if "listsd" in taps else "Internal").ap()
    yffn = nc.dram_tensor("yffn", [TOWN + CCAP + 8, C], F32,
                          kind="ExternalOutput" if "yffn" in taps else "Internal").ap()
    tapd = {}
    for tname, shape in (("kph0", [128, T]), ("kpair", [128, T]), ("qpair", [128, TOWN]),
                         ("vtiles", [128, 4160]), ("oall", [C, TOWN]),
                         ("accp3", [C, TOWN]), ("probe", [128, 512]),
                         ("comb", [8, TOWN])):
        if tname in taps:
            tapd[tname] = nc.dram_tensor("tap_" + tname, shape, F32,
                                         kind="ExternalOutput").ap()

    P = _Prog()
    es = contextlib.ExitStack()
    with es:
        sem = {e: es.enter_context(nc.semaphore("s_" + e)) for e in SEMS}

        def sb(name, shape):
            return es.enter_context(nc.sbuf_tensor(name, shape, F32))

        tO = sb("tO", [128, 16384])
        tXA = sb("tXA", [128, 16384])
        tK = sb("tK", [128, 4096])
        tV = sb("tV", [128, 4160])
        tQ = sb("tQ", [128, 2048])
        tP = sb("tP", [128, 2048])
        tSH = sb("tSH", [128, 1024])
        tW = sb("tW", [128, 3072])
        tM = sb("tM", [128, 192])
        cID = sb("cID", [128, 128])
        cBCO = sb("cBCO", [1, 128])
        cONE = sb("cONE", [128, 1])
        cALI = sb("cALI", [128, 512])
        cMA = sb("cMA", [128, 128])
        cMB = sb("cMB", [128, 128])
        cB1 = sb("cB1", [128, 64])
        cB3 = sb("cB3", [128, 64])
        cSB1 = sb("cSB1", [128, 8])
        cSB2 = sb("cSB2", [128, 8])
        cSB3 = sb("cSB3", [128, 8])
        cGW = sb("cGW", [128, 64])
        cTRI = sb("cTRI", [128, 144])
        cI128 = sb("cI128", [128, 1])
        cON2 = sb("cON2", [128, 2])
        cSLP = sb("cSLP", [2, NEXP])
        cSL2 = sb("cSL2", [2, 256])
        cSENT = es.enter_context(nc.sbuf_tensor("cSENT", [128, 5], mybir.dt.uint32))

        PS = [es.enter_context(nc.psum_tensor("ps%d" % i, [128, 1024], F32))
              for i in range(4)]

        OB = [_Buf() for _ in range(8)]
        XB = [_Buf() for _ in range(4)]
        KB = [_Buf() for _ in range(4)]
        VB = [_Buf() for _ in range(4)]
        QB = [_Buf() for _ in range(2)]
        PB = [_Buf() for _ in range(4)]
        SHB = [_Buf() for _ in range(2)]
        WB = [_Buf() for _ in range(3)]
        MB = _Buf()
        CB = _Buf()
        PSB = [_Buf() for _ in range(8)]
        XNF = _Buf()
        XNT = _Buf()
        CMD = _Buf()
        LB = [_Buf() for _ in range(NEXP)]
        PFH = [[_Buf() for _ in range(NEXP)] for _ in range(2)]
        YFB = _Buf()

        def bank(i):
            return PS[i // 2][:, (i % 2) * 512:(i % 2 + 1) * 512]

        def ob(c0, c1):
            return tuple(OB[c0 // 2048:(c1 - 1) // 2048 + 1])

        def dma(q, dst, src, r, w, stream=None, war_only=False):
            if stream is None:
                stream = "sW" if q == "sy" else "gX"
            P.emit(q, lambda e, a=dst, b=src: e.dma_start(out=a, in_=b),
                   r=r, w=w, step=16, stream=stream, war_only=war_only)

        def idma_gather(out, in_, idx, r, w):
            P.emit("gp", lambda e, o=out, i=in_, x=idx:
                   e.indirect_dma_start(out=o, out_offset=None, in_=i,
                                        in_offset=bass.IndirectOffsetOnAxis(ap=x, axis=0)),
                   r=r, w=w, step=16, stream="gI")

        def idma_scatter(out, in_, idx, r, w, accum=False, war_only=False):
            kw = {"compute_op": OP.add} if accum else {}
            P.emit("gp", lambda e, o=out, i=in_, x=idx, k=kw:
                   e.indirect_dma_start(out=o, out_offset=bass.IndirectOffsetOnAxis(ap=x, axis=0),
                                        in_=i, in_offset=None, **k),
                   r=r, w=w, step=16, stream="gI", war_only=war_only)

        def mm(out, lhsT, rhs, r, w, start=True, stop=True):
            P.emit("te",
                   lambda e, o=out, l=lhsT, m=rhs, st=start, sp=stop:
                   e.matmul(o, l, m, start=st, stop=sp, skip_group_check=True),
                   r=r, w=w)

        def tr(out, in_, idn, r, w, start=True):
            P.emit("te", lambda e, o=out, i=in_, d=idn, s=start:
                   e.matmul(o, i, d, start=s, stop=True, is_transpose=True,
                            skip_group_check=True), r=r, w=w)

        def ve_copy(out, in_, r, w):
            P.emit("ve", lambda e, o=out, i=in_: e.tensor_copy(o, i), r=r, w=w)

        def ve_tt(out, in0, in1, op, r, w):
            P.emit("ve", lambda e, o=out, a=in0, b=in1, p=op: e.tensor_tensor(o, a, b, p),
                   r=r, w=w)

        def ve_ts(out, in0, s1, op0, r, w, s2=None, op1=None):
            if op1 is None:
                P.emit("ve", lambda e, o=out, a=in0, s=s1, p=op0:
                       e.tensor_scalar(o, a, s, None, p), r=r, w=w)
            else:
                P.emit("ve", lambda e, o=out, a=in0, s=s1, t=s2, p=op0, q=op1:
                       e.tensor_scalar(o, a, s, t, p, q), r=r, w=w)

        def ve_stt(out, in0, s, in1, op0, op1, r, w):
            P.emit("ve", lambda e, o=out, a=in0, c=s, b=in1, p=op0, q=op1:
                   e.scalar_tensor_tensor(o, a, c, b, p, q), r=r, w=w)

        def ve_red(out, in_, op, r, w, absv=None, neg=None):
            if op == "max":
                P.emit("ve", lambda e, o=out, i=in_, a=absv, n=neg:
                       e.reduce_max(o, i, axis=AX.X, apply_absolute_value=a, negate=n),
                       r=r, w=w)
            else:
                P.emit("ve", lambda e, o=out, i=in_: e.reduce_sum(o, i, axis=AX.X),
                       r=r, w=w)

        def ve_recip(out, in_, r, w):
            P.emit("ve", lambda e, o=out, i=in_: e.reciprocal(o, i), r=r, w=w)

        def ve_memset(ap, const, w):
            P.emit("ve", lambda e, a=ap, c=const: e.memset(a, c), r=(), w=w)

        def act(out, in_, func, r, w, bias=None, scale=None):
            kw = {}
            if bias is not None:
                kw["bias"] = bias
            if scale is not None:
                kw["scale"] = scale
            P.emit("se", lambda e, o=out, i=in_, f=func, k=kw: e.activation(o, i, f, **k),
                   r=r, w=w)

        for dst, src in ((cID, ident), (cBCO, bco), (cONE, onescol), (cALI, alibi_in),
                         (cMA, maska_in), (cMB, maskb_in),
                         (cB1, b1c), (cB3, b3c), (cSB1, sb1c), (cSB2, sb2c),
                         (cSB3, sb3c), (cTRI, ctri_in), (cI128, ci128_in),
                         (cON2, cones2_in), (cSL2, csel2_in), (cSLP, cslopes_in)):
            dma("sy", dst[:, :], src, r=(), w=(CB,), stream="sC")
        dma("sy", cGW[:, :].rearrange("p (c g) -> p c g", g=8),
            gwT.rearrange("(c p) g -> p c g", p=128), r=(), w=(CB,), stream="sC")

        padf = tM[:, 160:161]
        P.emit("ve", lambda e: e.tensor_scalar(padf, cI128[:, :], float(CCAP + 1), None, OP.add),
               r=(CB,), w=(MB,))
        for c5 in range(5):
            P.emit("ve", lambda e, c=c5: e.tensor_scalar(tM[:, 161 + c:162 + c], cI128[:, :],
                                                         5.0, float(TOWN + c), OP.mult, OP.add),
                   r=(CB,), w=(MB,))
        ve_copy(cSENT[:, :], tM[:, 161:166], r=(MB,), w=(CB,))
        zb = tO[:, 0:1024]
        ve_memset(zb, 0.0, w=(OB[0],))
        for zi in range((TOWN + CCAP + 8) // 128 + 1):
            r0 = zi * 128
            r1 = min(r0 + 128, TOWN + CCAP + 8)
            if r0 >= r1:
                break
            dma("sy", yffn[r0:r1, :], zb[0:r1 - r0, :], r=(OB[0],), w=(YFB,), stream="sC")
        for zi in range(4):
            dma("sy", combd[TOWN + zi * 128: TOWN + (zi + 1) * 128, :], zb[:, 0:8],
                r=(OB[0],), w=(CMD,), stream="sC")

        x4 = [(tK[:, 0:4096], tuple(KB)), (tV[:, 0:4096], tuple(VB))]
        sq4 = (tO[:, 10240:14336], ob(10240, 14336))
        stage = tO[:, 2048:10240]
        stB = ob(2048, 10240)
        for grp in range(NBLK // 4):
            sl = grp % 2
            xb, xbB = x4[sl]
            sq_, sqB = sq4
            for k in range(4):
                tt = grp * 4 + k
                dma("gp", xb[:, k * 1024:(k + 1) * 1024], x[tt * 128:(tt + 1) * 128, :],
                    r=xbB, w=xbB)
            s4 = tM[:, 20 + sl * 16: 24 + sl * 16]
            mean4 = tM[:, 24 + sl * 16: 28 + sl * 16]
            var4 = tM[:, 28 + sl * 16: 32 + sl * 16]
            rst4 = tM[:, 32 + sl * 16: 36 + sl * 16]

            def s4v(ap):
                return ap.rearrange("p (t o) -> p t o", o=1)

            def x4v(ap):
                return ap.rearrange("p (t c) -> p t c", c=C)

            P.emit("ve", lambda e, a=s4v(s4), b=x4v(xb): e.tensor_reduce(a, b, axis=AX.X,
                                                                         op=OP.add),
                   r=xbB, w=(MB,))
            ve_ts(mean4, s4, 1.0 / C, OP.mult, r=(MB,), w=(MB,))
            P.emit("ve", lambda e, a=x4v(xb), m=s4v(mean4).broadcast_to((128, 4, C)):
                   e.tensor_tensor(a, a, m, OP.subtract), r=xbB + (MB,), w=xbB)
            act(sq_, xb, AF.Square, r=xbB + sqB, w=sqB)
            P.emit("ve", lambda e, a=s4v(s4), b=x4v(sq_): e.tensor_reduce(a, b, axis=AX.X,
                                                                          op=OP.add),
                   r=sqB, w=(MB,))
            ve_ts(var4, s4, 1.0 / C, OP.mult, r=(MB,), w=(MB,), s2=1e-5, op1=OP.add)
            act(var4, var4, AF.Sqrt, r=(MB,), w=(MB,))
            ve_recip(rst4, var4, r=(MB,), w=(MB,))
            P.emit("ve", lambda e, a=x4v(xb), m=s4v(rst4).broadcast_to((128, 4, C)):
                   e.tensor_tensor(a, a, m, OP.mult), r=xbB + (MB,), w=xbB)
            for k in range(4):
                tt = grp * 4 + k
                xn_ = xb[:, k * 1024:(k + 1) * 1024]
                ps = PS[tt % 2]
                psb = (PSB[2 * (tt % 2)], PSB[2 * (tt % 2) + 1])
                for cc in range(NCC):
                    tr(ps[:, cc * 128:(cc + 1) * 128], xn_[:, cc * 128:(cc + 1) * 128],
                       cID[:, :], r=xbB + (CB,), w=(psb[cc // 4],), start=(cc % 4 == 0))
                k8 = tt % 8
                stv = stage.rearrange("p (c t) -> p c t", t=1024)[:, :, k8 * 128:(k8 + 1) * 128]
                P.emit("ve", lambda e, o=stv, i=ps[:, :1024].rearrange("p (c t) -> p c t", t=128):
                       e.tensor_copy(o, i), r=psb + stB, w=stB)
                if tt % 2 == 0:
                    jown = tt // 2
                    dma("gp", xntd[jown * 128:(jown + 1) * 128, :], xn_, r=xbB, w=(XNT,),
                        stream="gS", war_only=True)
            if grp % 2 == 1:
                g8 = grp // 2
                for cc in range(NCC):
                    dma("gp", xnf[cc * 128:(cc + 1) * 128, g8 * 1024:(g8 + 1) * 1024],
                        stage[:, cc * 1024:(cc + 1) * 1024], r=stB, w=(XNF,),
                        stream="gS", war_only=True)
        P.barrier()

        for sc in range(32):
            ve_memset(tV[:, sc * 130 + 64: sc * 130 + 65], 1.0, w=(VB[(sc * 130) // 1040],))
            ve_memset(tV[:, sc * 130 + 129: sc * 130 + 130], 1.0, w=(VB[(sc * 130 + 129) // 1040],))

        kpair = tK[:, :]
        kpB = KB
        qpair = tQ[:, :]
        qpB = QB

        def vtile(sc):
            return tV[:, sc * 130:(sc + 1) * 130]

        def vtB(sc):
            return VB[(sc * 130) // 1040]

        xach = [tXA[:, i * 4096:(i + 1) * 4096] for i in range(4)]
        sqa = tXA[:, 0:4096]
        normq2 = tXA[0:1, 4096:6144]
        shiftrow = [tXA[0:1, 6144:8192], tXA[0:1, 8192:10240]]
        normk2 = tXA[0:1, 10240:14336]
        iotarow = tXA[0:1, 10240:12288]
        sloperow = tXA[0:1, 12288:14336]
        maxk2 = tM[0:1, 130:131]
        maxk8 = tM[0:1, 131:132]
        opair = tM[:, 0:128]
        recc = [tM[:, 128:129], tM[:, 129:130]]
        wqb = (tW[:, 0:1024], WB[0])
        wkb = (tW[:, 1024:2048], WB[1])
        wvb = (tW[:, 2048:3072], WB[2])
        pbuf = [[(tP[:, 0:512], PB[0]), (tP[:, 512:1024], PB[1])],
                [(tP[:, 1024:1536], PB[2]), (tP[:, 1536:2048], PB[3])]]
        shbc = [(tSH[:, 0:512], SHB[0]), (tSH[:, 512:1024], SHB[1])]

        for pair in range(8):
            for wb_, wsrc in ((wqb, wqT), (wkb, wkT), (wvb, wvT)):
                dma("sy", wb_[0].rearrange("p (c d) -> p c d", d=128),
                    wsrc[:, pair * 128:(pair + 1) * 128].rearrange("(c p) d -> p c d", p=128),
                    r=(wb_[1],), w=(wb_[1],))
            for hidx2, half in enumerate(range(2)):
                iord = (2, 3, 0, 1) if half == 0 else (0, 1, 2, 3)
                for i in range(4):
                    cc = half * 4 + i
                    if half == 0 and pair > 0 and i >= 2:
                        continue
                    dma("gp", xach[i], xnf[cc * 128:(cc + 1) * 128, :], r=(XB[i], XNF), w=(XB[i],))
                for ii, i in enumerate(iord):
                    for scp in range(8):
                        mm(bank(scp), wkb[0][:, (half * 4 + i) * 128:(half * 4 + i + 1) * 128],
                           xach[i][:, scp * 512:(scp + 1) * 512],
                           r=(wkb[1], XB[i]), w=(PSB[scp],), start=(ii == 0), stop=(ii == 3))
                for scp in range(8):
                    dst = kpair[:, scp * 512:(scp + 1) * 512]
                    bref = kpB[scp // 2]
                    if half == 0:
                        ve_copy(dst, bank(scp), r=(PSB[scp],), w=(bref,))
                    else:
                        ve_tt(dst, dst, bank(scp), OP.add, r=(PSB[scp], bref), w=(bref,))
                if "kph0" in tapd and pair == 0 and half == 0:
                    dma("gp", tapd["kph0"], kpair, r=tuple(kpB), w=())
                for ii, i in enumerate(iord):
                    for scp in range(8):
                        mm(bank(scp), wvb[0][:, (half * 4 + i) * 128:(half * 4 + i + 1) * 128],
                           xach[i][:, scp * 512:(scp + 1) * 512],
                           r=(wvb[1], XB[i]), w=(PSB[scp],), start=(ii == 0), stop=(ii == 3))
                for scp in range(8):
                    dst = tV[:, scp * 512:(scp + 1) * 512]
                    if hidx2 == 0:
                        ve_copy(dst, bank(scp), r=(PSB[scp],) + tuple(VB), w=tuple(VB))
                    else:
                        ve_tt(dst, dst, bank(scp), OP.add, r=(PSB[scp],) + tuple(VB),
                              w=tuple(VB))
                for ii, i in enumerate(iord):
                    for qt in range(4):
                        rhs = xach[i].rearrange("p (b c) -> p b c", c=128)[:, 8 * qt: 8 * qt + 8: 2, :]
                        mm(bank(qt), wqb[0][:, (half * 4 + i) * 128:(half * 4 + i + 1) * 128],
                           rhs, r=(wqb[1], XB[i]), w=(PSB[qt],), start=(ii == 0), stop=(ii == 3))
                for qt in range(4):
                    dst = qpair[:, qt * 512:(qt + 1) * 512]
                    bref = qpB[qt // 2]
                    if half == 0:
                        ve_copy(dst, bank(qt), r=(PSB[qt],), w=(bref,))
                    else:
                        ve_tt(dst, dst, bank(qt), OP.add, r=(PSB[qt], bref), w=(bref,))

            for sc in range(31, -1, -1):
                bk = sc % 2
                tr(bank(bk)[:, 0:128], tV[:, sc * 128:(sc + 1) * 128], cID[:, :],
                   r=tuple(VB) + (CB,), w=(PSB[bk],))
                vt = vtile(sc)
                dst = vt[:, 0:130].rearrange("p (g d) -> p g d", d=65)[:, :, 0:64]
                srcv = bank(bk)[:, 0:128].rearrange("p (g d) -> p g d", d=64)
                ve_copy(dst, srcv, r=(PSB[bk],) + tuple(VB), w=tuple(VB))
            for oc in (64, 129):
                ve_memset(tV[:, 0:4160].rearrange("p (s c) -> p s c", c=130)[:, :, oc:oc + 1],
                          1.0, w=tuple(VB))

            if "probe" in tapd and pair == 0:
                dma("gp", tapd["probe"], sqa[:, 0:512], r=(XB[0],), w=())
            if "kpair" in tapd and pair == 0:
                dma("gp", tapd["kpair"], kpair, r=tuple(kpB), w=())
            if "qpair" in tapd and pair == 0:
                dma("gp", tapd["qpair"], qpair, r=tuple(qpB), w=())
            if "vtiles" in tapd and pair == 0:
                dma("gp", tapd["vtiles"], tV[:, :], r=tuple(VB), w=())

            normq2f = tXA[0:2, 4096:6144]
            srow2 = tXA[0:2, 6144:8192]
            iota2 = tXA[0:2, 8192:10240]
            normk2f = tXA[0:2, 10240:14336]
            slope2 = tXA[0:2, 12288:14336]
            maxk2b = tM[0:2, 130:131]
            maxk8b = tM[0:2, 131:132]
            act(sqa[:, 0:4096], kpair[:, :], AF.Square, r=tuple(kpB), w=(XB[0],))
            for ch in range(8):
                mm(bank(ch)[0:2, :], cON2[:, :], sqa[:, ch * 512:(ch + 1) * 512],
                   r=(CB, XB[0]), w=(PSB[ch],))
            for i4 in range(4):
                ve_copy(normk2f[:, i4 * 1024:(i4 + 1) * 1024], PS[i4][0:2, :],
                        r=(PSB[2 * i4], PSB[2 * i4 + 1]), w=(XB[2], XB[3]))
            ve_red(maxk2b, normk2f, "max", r=(XB[2], XB[3]), w=(MB,))
            act(maxk8b, maxk2b, AF.Sqrt, r=(MB,), w=(MB,), scale=1.0 / 64.0)
            act(sqa[:, 0:2048], qpair[:, :], AF.Square, r=tuple(qpB) + (XB[0],), w=(XB[0],))
            for ch in range(4):
                mm(bank(ch)[0:2, :], cON2[:, :], sqa[:, ch * 512:(ch + 1) * 512],
                   r=(CB, XB[0]), w=(PSB[ch],))
            for i4 in range(2):
                ve_copy(normq2f[:, i4 * 1024:(i4 + 1) * 1024], PS[i4][0:2, :],
                        r=(PSB[2 * i4], PSB[2 * i4 + 1]), w=(XB[1],))
            ve_ts(srow2, normq2f, 1e-30, OP.add, r=(XB[1],), w=(XB[1],))
            act(srow2, srow2, AF.Sqrt, r=(XB[1],), w=(XB[1],))
            for r2 in range(2):
                dma("gp", iota2[r2:r2 + 1, :], iota_in, r=(XB[2],), w=(XB[2],))
            ve_tt(slope2, iota2, cSLP[:, pair:pair + 1].broadcast_to((2, TOWN)),
                  OP.mult, r=(XB[2], XB[3], CB), w=(XB[3],))
            ve_tt(srow2, srow2, maxk8b.broadcast_to((2, TOWN)), OP.mult,
                  r=(XB[1], MB), w=(XB[1],))
            ve_tt(srow2, srow2, slope2, OP.add, r=(XB[1], XB[3]), w=(XB[1],))
            if pair < 7:
                for i in (2, 3):
                    dma("gp", xach[i], xnf[i * 128:(i + 1) * 128, :],
                        r=(XB[i], XNF), w=(XB[i],), stream="gS")

            for qt in range(4):
                for hh in range(2):
                    mm(bank(6), cSL2[:, hh * 128:(hh + 1) * 128],
                       srow2[:, qt * 512:(qt + 1) * 512],
                       r=(CB, XB[1]), w=(PSB[6],))
                    ve_copy(shbc[hh][0], bank(6), r=(PSB[6],), w=(shbc[hh][1],))
                scmax = 8 * qt + 7
                av_bank = (PSB[4], PSB[5])
                pend = None

                def emit_av(sc, js):
                    for hh in range(2):
                        pb, pbB = pbuf[hh][sc % 2]
                        mm(bank(4 + hh)[0:65, js * 128:512],
                           vtile(sc)[:, hh * 65:(hh + 1) * 65],
                           pb[:, js * 128:512],
                           r=(pbB, vtB(sc)), w=(av_bank[hh],),
                           start=(sc == 0), stop=(sc == scmax))

                for sc in range(scmax + 1):
                    js = min(j for j in range(4) if sc <= 2 * (4 * qt + j) + 1)
                    w_ = 512 - js * 128
                    for hh in range(2):
                        h = 2 * pair + hh
                        rows = slice(hh * 64, (hh + 1) * 64)
                        bkid = (sc % 3) if hh == 0 else (3, 6, 7)[sc % 3]
                        sB = PSB[bkid]
                        sap = bank(bkid)
                        mm(sap[:, js * 128: 512], kpair[rows, sc * 128:(sc + 1) * 128],
                           qpair[rows, qt * 512 + js * 128:(qt + 1) * 512],
                           r=tuple(kpB) + tuple(qpB), w=(sB,))
                        ve_stt(sap[:, js * 128:512], sap[:, js * 128:512], 0.125,
                               shbc[hh][0][:, js * 128:512], OP.mult, OP.subtract,
                               r=(sB, shbc[hh][1]), w=(sB,))
                        for j in range(js, 4):
                            d = sc - 2 * (4 * qt + j)
                            if d == 0:
                                mk = cMA
                            elif d == 1:
                                mk = cMB
                            else:
                                continue
                            ve_tt(sap[:, j * 128:(j + 1) * 128], sap[:, j * 128:(j + 1) * 128],
                                  mk[:, :], OP.subtract, r=(sB, CB), w=(sB,))
                        pb, pbB = pbuf[hh][sc % 2]
                        act(pb[:, js * 128:512], sap[:, js * 128:512], AF.Exp,
                            r=(sB, CB), w=(pbB,), bias=cALI[:, h * 32 + sc: h * 32 + sc + 1])
                    if pend is not None:
                        emit_av(*pend)
                    pend = (sc, js)
                emit_av(*pend)

                ocols = slice(pair * 2048 + qt * 512, pair * 2048 + (qt + 1) * 512)
                dn2 = tXA[0:1, 0:4096]
                for hh in range(2):
                    ve_copy(dn2[0:1, hh * 2048 + qt * 512: hh * 2048 + (qt + 1) * 512],
                            bank(4 + hh)[64:65, :], r=(av_bank[hh],), w=(XB[0],))
                    if hh == 0:
                        ve_copy(tO[0:64, ocols], bank(4)[0:64, :],
                                r=(av_bank[0],), w=(OB[pair],))
                    else:
                        stg = tP[0:64, 0:512]
                        ve_copy(stg, bank(5)[0:64, :], r=(av_bank[1],), w=(PB[0],))
                        dma("gp", tO[64:128, ocols], stg, r=(PB[0],), w=(OB[pair],))

            dn2 = tXA[0:1, 0:4096]
            P.emit("ve", lambda e, a=dn2: e.reciprocal(a, a), r=(XB[0],), w=(XB[0],))
            for q4 in range(4):
                for hh in range(2):
                    oc = slice(pair * 2048 + q4 * 512, pair * 2048 + (q4 + 1) * 512)
                    bk = 4 + hh
                    mm(bank(bk), cBCO[0:1, :],
                       dn2[0:1, hh * 2048 + q4 * 512: hh * 2048 + (q4 + 1) * 512],
                       r=(CB, XB[0]), w=(PSB[bk],))
                    if hh == 0:
                        ve_tt(tO[0:64, oc], tO[0:64, oc], bank(4)[0:64, :], OP.mult,
                              r=(OB[pair], PSB[4]), w=(OB[pair],))
                    else:
                        ve_tt(tO[64:128, oc], tO[64:128, oc], bank(5)[64:128, :], OP.mult,
                              r=(OB[pair], PSB[5]), w=(OB[pair],))
        if "oall" in tapd:
            for cc in range(8):
                dma("gp", tapd["oall"][cc * 128:(cc + 1) * 128, :],
                    tO[:, cc * 2048:(cc + 1) * 2048], r=(OB[cc],), w=())
        P.barrier()

        acc = [tXA[:, dc * 2048:(dc + 1) * 2048] for dc in range(8)]
        accB = [XB[dc // 2] for dc in range(8)]
        h1s = [(tO[:, ic * 1024:(ic + 1) * 1024], OB[ic // 2]) for ic in range(8)]
        h3s = [(tO[:, 8192 + ic * 1024: 8192 + (ic + 1) * 1024], OB[4 + ic // 2]) for ic in range(8)]
        wfb = [(tP[:, 0:1024], (PB[0], PB[1])), (tW[:, 0:1024], (WB[0],))]
        gwrk = tP[:, 1024:2048]
        gwB = (PB[2], PB[3])
        tmpz = (tQ[:, 1024:1536], QB[1])
        cB2 = tW[0:8, 2048:3072]
        dma("sy", cB2, b2, r=(WB[2],), w=(WB[2],))
        combT2 = [(tSH[0:8, 0:1024], SHB[0]), (tW[0:8, 1024:2048], WB[1])]
        logi = (tQ[0:8, 0:1024], QB[0])
        cums = (tQ[0:8, 1024:2048], QB[1])
        h1sp = [tO[:, ic * CCAP:(ic + 1) * CCAP] for ic in range(8)]
        h3sp = [tO[:, 3072 + ic * CCAP: 3072 + (ic + 1) * CCAP] for ic in range(8)]
        xgt = [tO[:, 6144 + i * 1024: 6144 + (i + 1) * 1024] for i in range(3)]
        xgf = [tO[:, 9216 + cc * CCAP: 9216 + (cc + 1) * CCAP] for cc in range(8)]
        ztok = [tO[:, 12288 + i * 1024: 12288 + (i + 1) * 1024] for i in range(3)]

        ltok = tP[:, 1024:1088]
        pshift = tP[:, 1088:1152]
        mask_all = tP[:, 1152:1216]
        pos_all = tP[:, 1216:1280]
        offu64 = tP[:, 1280:1344]
        p8n_t = tP[:, 1344:1408]
        p8_t = tP[:, 1408:1472]
        cmb_all = tP[:, 1472:1536]
        mg8 = tP[:, 1536:1544]
        sg8 = tP[:, 1544:1552]
        m8b = tP[:, 1552:1560]
        viof8 = tP[:, 1560:1568]
        viou8 = tP[:, 1568:1576]
        basetile = tP[:, 1576:1584]
        counts_sb = tM[0:8, 168:176]
        countsT = tM[0:8, 176:184]
        offs_sb = tM[0:8, 184:192]

        def seg(ap, e=8):
            return ap.rearrange("p (g e) -> p g e", e=e)

        def segb(ap8):
            return ap8.rearrange("p (g o) -> p g o", o=1).broadcast_to((128, 8, 8))

        def emit_routing(th):
            combT = combT2[th]
            for cc in range(8):
                xb_ = wfb[cc % 2]
                src2 = xnf[cc * 128:(cc + 1) * 128, :].rearrange("p (b c) -> p b c", c=128)
                src2 = src2[:, 16 * th: 16 * (th + 1): 2, :]
                dma("gp", xb_[0].rearrange("p (b c) -> p b c", c=128), src2,
                    r=xb_[1] + (XNF,), w=xb_[1])
                for tc in range(2):
                    mm(bank(2 + tc)[0:8, :], cGW[:, cc * 8:(cc + 1) * 8],
                       xb_[0][:, tc * 512:(tc + 1) * 512],
                       r=(CB,) + xb_[1], w=(PSB[2 + tc],), start=(cc == 0), stop=(cc == 7))
            for tc in range(2):
                ve_copy(logi[0][:, tc * 512:(tc + 1) * 512], bank(2 + tc)[0:8, :],
                        r=(PSB[2 + tc],), w=(logi[1],))
            for g in range(8):
                tr(bank(7)[:, g * 8:(g + 1) * 8], logi[0][:, g * 128:(g + 1) * 128],
                   cID[0:8, 0:8], r=(logi[1], CB), w=(PSB[7],), start=(g == 0))
            ve_copy(ltok, bank(7)[:, 0:64], r=(PSB[7],), w=gwB)
            P.emit("ve", lambda e: e.tensor_reduce(seg(mg8, 1), seg(ltok), axis=AX.X,
                                                   op=OP.max, negate=True),
                   r=gwB, w=gwB)
            ve_tt(seg(pshift), seg(ltok), segb(mg8), OP.add, r=gwB, w=gwB)
            act(p8_t, pshift, AF.Exp, r=gwB, w=gwB)
            P.emit("ve", lambda e: e.tensor_reduce(seg(sg8, 1), seg(p8_t), axis=AX.X,
                                                   op=OP.add), r=gwB, w=gwB)
            ve_recip(sg8, sg8, r=gwB, w=gwB)
            ve_tt(seg(p8n_t), seg(p8_t), segb(sg8), OP.mult, r=gwB, w=gwB)
            P.emit("ve", lambda e: e.tensor_reduce(seg(m8b, 1), seg(p8n_t), axis=AX.X,
                                                   op=OP.max), r=gwB, w=gwB)
            ve_tt(seg(pshift), seg(p8n_t), segb(m8b), OP.is_ge, r=gwB, w=gwB)
            ve_stt(p8_t, pshift, -BIG, p8n_t, OP.mult, OP.add, r=gwB, w=gwB)
            P.emit("ve", lambda e: e.tensor_reduce(seg(m8b, 1), seg(p8_t), axis=AX.X,
                                                   op=OP.max), r=gwB, w=gwB)
            ve_tt(seg(p8_t), seg(p8_t), segb(m8b), OP.is_ge, r=gwB, w=gwB)
            ve_tt(ltok, pshift, p8_t, OP.add, r=gwB, w=gwB)
            ve_tt(cmb_all, ltok, p8n_t, OP.mult, r=gwB, w=gwB)
            dma("gp", combd[th * 1024:(th + 1) * 1024, :].rearrange("(g p) e -> p g e", p=128),
                seg(cmb_all), r=gwB, w=(CMD,), stream="gI")
            for g in range(8):
                tr(bank(6 if g < 4 else 5)[0:8, (g % 4) * 128:(g % 4 + 1) * 128],
                   cmb_all[:, g * 8:(g + 1) * 8], cID[:, :],
                   r=gwB + (CB,), w=(PSB[6 if g < 4 else 5],), start=(g % 4 == 0))
            ve_copy(combT[0][:, 0:512], bank(6)[0:8, :], r=(PSB[6],), w=(combT[1],))
            ve_copy(combT[0][:, 512:1024], bank(5)[0:8, :], r=(PSB[5],), w=(combT[1],))
            ve_ts(mask_all, cmb_all, 0.0, OP.is_gt, r=gwB, w=gwB)
            for g in range(8):
                bk = 7 if g < 4 else 4
                mm(bank(bk)[0:8, (g % 4) * 128:(g % 4 + 1) * 128],
                   mask_all[:, g * 8:(g + 1) * 8], cTRI[:, 0:128],
                   r=gwB + (CB,), w=(PSB[bk],), start=(g % 4 == 0))
            ve_copy(cums[0][:, 0:512], bank(7)[0:8, :], r=(PSB[7],), w=(cums[1],))
            ve_copy(cums[0][:, 512:1024], bank(4)[0:8, :], r=(PSB[4],), w=(cums[1],))
            ve_copy(counts_sb,
                    cums[0].rearrange("e (g t) -> e g t", t=128)[:, :, 127],
                    r=(cums[1],), w=(MB,))
            tr(bank(7)[0:8, 0:8], counts_sb, cID[0:8, 0:8], r=(MB, CB), w=(PSB[7],))
            ve_copy(countsT, bank(7)[0:8, 0:8], r=(PSB[7],), w=(MB,))
            mm(bank(7)[0:8, 0:8], countsT, cTRI[0:8, 128:136], r=(MB, CB), w=(PSB[7],))
            ve_copy(offs_sb, bank(7)[0:8, 0:8], r=(PSB[7],), w=(MB,))
            for e_ in range(NEXP):
                lb = (th * NEXP + e_) * 640
                dma("sy", listsd[lb:lb + 640, 0].rearrange("(p c) -> p c", c=5),
                    cSENT[:, :], r=(CB,), w=(LB[e_], PFH[th][e_]), stream="sC",
                    war_only=True)
                ve_memset(basetile[:, e_:e_ + 1], float(lb), w=gwB)
            P.emit("ve", lambda e: e.tensor_tensor(
                cums[0].rearrange("e (g t) -> e g t", t=128),
                cums[0].rearrange("e (g t) -> e g t", t=128),
                offs_sb.rearrange("e (g o) -> e g o", o=1).broadcast_to((8, 8, 128)),
                OP.add), r=(cums[1], MB), w=(cums[1],))
            for g in range(8):
                bk = 7 if g < 4 else 4
                tr(bank(bk)[:, (g % 4) * 8:(g % 4 + 1) * 8],
                   cums[0][:, g * 128:(g + 1) * 128], cID[0:8, 0:8],
                   r=(cums[1], CB), w=(PSB[bk],), start=(g % 4 == 0))
            ve_copy(pos_all[:, 0:32], bank(7)[:, 0:32], r=(PSB[7],), w=gwB)
            ve_copy(pos_all[:, 32:64], bank(4)[:, 0:32], r=(PSB[4],), w=gwB)
            ve_ts(ltok, pos_all, -1.0, OP.add, r=gwB, w=gwB, s2=float(CCAP), op1=OP.min)
            ve_tt(pshift, ltok, padf.broadcast_to((128, 64)), OP.subtract,
                  r=gwB + (MB,), w=gwB)
            ve_tt(pshift, pshift, mask_all, OP.mult, r=gwB, w=gwB)
            ve_tt(pshift, pshift, padf.broadcast_to((128, 64)), OP.add,
                  r=gwB + (MB,), w=gwB)
            P.emit("ve", lambda e: e.tensor_tensor(
                seg(pshift), seg(pshift),
                basetile.rearrange("p (o e) -> p o e", o=1).broadcast_to((128, 8, 8)),
                OP.add), r=gwB, w=gwB)
            ve_copy(offu64.bitcast(mybir.dt.uint32), pshift, r=gwB, w=gwB)
            for g in range(8):
                ve_ts(viof8[:, g:g + 1], cI128[:, :], float(th * 1024 + g * 128), OP.add,
                      r=(CB,), w=gwB)
            ve_copy(viou8.bitcast(mybir.dt.uint32), viof8, r=gwB, w=gwB)
            for g in range(8):
                for e_ in range(NEXP):
                    idma_scatter(listsd, viou8.bitcast(mybir.dt.uint32)[:, g:g + 1],
                                 offu64.bitcast(mybir.dt.uint32)[:, g * 8 + e_: g * 8 + e_ + 1],
                                 r=gwB + (PFH[th][e_],), w=(LB[e_],), war_only=True)
            if "comb" in tapd:
                dma("gp", tapd["comb"][:, th * 1024:(th + 1) * 1024], combT[0],
                    r=(combT[1],), w=())

        wo_tiles = []
        for cc in range(8):
            if cc < 4:
                ap, bref = tK[:, cc * 1024:(cc + 1) * 1024], KB[cc]
            else:
                ap, bref = tV[:, (cc - 4) * 1024:(cc - 3) * 1024], VB[cc - 4]
            wo_tiles.append((ap, bref))
            dma("sy", ap, woT[cc * 128:(cc + 1) * 128, :], r=(bref,), w=(bref,))

        def emit_oproj(dcs):
            for dc in dcs:
                for tc in range(4):
                    pbk = PSB[tc % 2]
                    for cc in range(8):
                        mm(bank(tc % 2), wo_tiles[cc][0][:, dc * 128:(dc + 1) * 128],
                           tO[:, cc * 2048 + tc * 512: cc * 2048 + (tc + 1) * 512],
                           r=(wo_tiles[cc][1], OB[cc]), w=(pbk,),
                           start=(cc == 0), stop=(cc == 7))
                    ve_copy(acc[dc][:, tc * 512:(tc + 1) * 512], bank(tc % 2),
                            r=(pbk,), w=(accB[dc],))

        emit_routing(0)
        emit_oproj(range(0, 3))
        emit_routing(1)
        emit_oproj(range(3, 8))
        for th_ in range(2):
            for dc in range(8):
                for tc in range(2):
                    mm(bank(2 + tc), cB2[0:8, dc * 128:(dc + 1) * 128],
                       combT2[th_][0][:, tc * 512:(tc + 1) * 512],
                       r=(WB[2], combT2[th_][1]), w=(PSB[2 + tc],))
                    aslc = acc[dc][:, th_ * 1024 + tc * 512: th_ * 1024 + (tc + 1) * 512]
                    ve_tt(aslc, aslc, bank(2 + tc), OP.add,
                          r=(accB[dc], PSB[2 + tc]), w=(accB[dc],))
        if "accp3" in tapd:
            for dc in range(8):
                dma("gp", tapd["accp3"][dc * 128:(dc + 1) * 128, :], acc[dc],
                    r=(accB[dc],), w=())
        P.barrier()

        xnh = []
        for cc in range(8):
            if cc < 4:
                xnh.append((tK[:, cc * 1024:(cc + 1) * 1024], KB[cc]))
            else:
                xnh.append((tV[:, (cc - 4) * 1024:(cc - 3) * 1024], VB[cc - 4]))

        for th in range(2):
            U32 = mybir.dt.uint32
            lcol2 = [tP[:, 1592:1595], tP[:, 1600:1603]]
            gcol2 = [tP[:, 1596:1599], tP[:, 1604:1607]]
            cgt2 = [[tO[:, 15360 + i * 8: 15368 + i * 8] for i in range(3)],
                    [tO[:, 15384 + i * 8: 15392 + i * 8] for i in range(3)]]

            def emit_route_load(e):
                sl = e % 2
                lb = (th * NEXP + e) * 640
                dma("gp", lcol2[sl].bitcast(U32),
                    listsd[lb:lb + CCAP, 0].rearrange("(p c) -> p c", c=3),
                    r=(LB[e], PB[3]), w=(PB[3],), stream="gI")
                ve_ts(gcol2[sl].bitcast(U32), lcol2[sl].bitcast(U32),
                      float(TOWN - 1), OP.min, r=(PB[3],), w=(PB[3],))
                for i in range(3):
                    idma_gather(xgt[i], xntd, gcol2[sl].bitcast(U32)[:, i:i + 1],
                                r=(XNT, PB[3]) + ob(6144 + i * 1024, 6144 + (i + 1) * 1024),
                                w=ob(6144 + i * 1024, 6144 + (i + 1) * 1024))
                    idma_gather(cgt2[sl][i], combd, lcol2[sl].bitcast(U32)[:, i:i + 1],
                                r=(CMD, PB[3]) + ob(15360, 15408),
                                w=ob(15360 + sl * 24 + i * 8, 15368 + sl * 24 + i * 8))

            def emit_xg_tr(e):
                for i in range(3):
                    for cc in range(8):
                        bk = 4 + (i * 8 + cc) % 2
                        tr(bank(bk)[:, 0:128], xgt[i][:, cc * 128:(cc + 1) * 128], cID[:, :],
                           r=ob(6144 + i * 1024, 6144 + (i + 1) * 1024) + (CB,), w=(PSB[bk],))
                        ve_copy(xgf[cc][:, i * 128:(i + 1) * 128], bank(bk)[:, 0:128],
                                r=(PSB[bk],),
                                w=ob(9216 + cc * CCAP, 9216 + (cc + 1) * CCAP))

            emit_route_load(0)
            for cc in range(8):
                src = xnf[cc * 128:(cc + 1) * 128, :].rearrange("p (b c) -> p b c", c=128)
                src = src[:, 16 * th: 16 * (th + 1): 2, :]
                dma("gp", xnh[cc][0].rearrange("p (b c) -> p b c", c=128), src,
                    r=(xnh[cc][1], XNF), w=(xnh[cc][1],))
            emit_xg_tr(0)
            for e in range(NEXP):
                sl = e % 2
                for mat, hdst, is1 in ((w1[e], h1sp, True), (w3[e], h3sp, False)):
                    for ic in range(8):
                        wb_ = wfb[ic % 2]
                        dma("sy", wb_[0].rearrange("p (c d) -> p c d", d=128),
                            mat[:, ic * 128:(ic + 1) * 128].rearrange("(c p) d -> p c d", p=128),
                            r=wb_[1], w=wb_[1])
                        pb2 = (2 if not is1 else 0) + (ic % 2)
                        for cc in range(8):
                            mm(bank(pb2)[:, 0:CCAP], wb_[0][:, cc * 128:(cc + 1) * 128],
                               xgf[cc],
                               r=wb_[1] + ob(9216 + cc * CCAP, 9216 + (cc + 1) * CCAP),
                               w=(PSB[pb2],), start=(cc == 0), stop=(cc == 7))
                        bias_ap = (cB1 if is1 else cB3)[:, e * 8 + ic: e * 8 + ic + 1]
                        base = 0 if is1 else 3072
                        act(hdst[ic], bank(pb2)[:, 0:CCAP],
                            AF.Silu if is1 else AF.Identity,
                            r=(PSB[pb2], CB),
                            w=ob(base + ic * CCAP, base + (ic + 1) * CCAP), bias=bias_ap)
                if e + 1 < NEXP:
                    emit_route_load(e + 1)
                for ic in range(8):
                    ve_tt(h1sp[ic], h1sp[ic], h3sp[ic], OP.mult,
                          r=ob(ic * CCAP, (ic + 1) * CCAP) + ob(3072 + ic * CCAP, 3072 + (ic + 1) * CCAP),
                          w=ob(ic * CCAP, (ic + 1) * CCAP))
                for dc in range(8):
                    wb_ = wfb[dc % 2]
                    dma("sy", wb_[0].rearrange("p (c d) -> p c d", d=128),
                        w2[e][:, dc * 128:(dc + 1) * 128].rearrange("(c p) d -> p c d", p=128),
                        r=wb_[1], w=wb_[1])
                    pb2 = dc % 2
                    for ic in range(8):
                        mm(bank(pb2)[:, 0:CCAP], wb_[0][:, ic * 128:(ic + 1) * 128],
                           h1sp[ic], r=wb_[1] + ob(ic * CCAP, (ic + 1) * CCAP),
                           w=(PSB[pb2],), start=(ic == 0), stop=(ic == 7))
                    ve_copy(h3sp[dc], bank(pb2)[:, 0:CCAP], r=(PSB[pb2],),
                            w=ob(3072 + dc * CCAP, 3072 + (dc + 1) * CCAP))
                if e + 1 < NEXP:
                    emit_xg_tr(e + 1)
                for i in range(3):
                    for dc in range(8):
                        bk = 6 + dc % 2
                        tr(bank(bk)[:, 0:128], h3sp[dc][:, i * 128:(i + 1) * 128],
                           cID[:, :],
                           r=ob(3072 + dc * CCAP, 3072 + (dc + 1) * CCAP) + (CB,),
                           w=(PSB[bk],), start=True)
                        ve_copy(ztok[i][:, dc * 128:(dc + 1) * 128], bank(bk)[:, 0:128],
                                r=(PSB[bk],),
                                w=ob(12288 + i * 1024, 12288 + (i + 1) * 1024))
                    ve_tt(ztok[i], ztok[i],
                          cgt2[sl][i][:, e:e + 1].broadcast_to((128, 1024)),
                          OP.mult,
                          r=ob(12288 + i * 1024, 12288 + (i + 1) * 1024)
                            + ob(15360 + sl * 24 + i * 8, 15368 + sl * 24 + i * 8),
                          w=ob(12288 + i * 1024, 12288 + (i + 1) * 1024))
                    idma_scatter(yffn, ztok[i], lcol2[sl].bitcast(U32)[:, i:i + 1],
                                 r=ob(12288 + i * 1024, 12288 + (i + 1) * 1024) + (PB[3], YFB),
                                 w=(YFB,), accum=True)

            for e in (NEXP,):
                wsrc1, wsrc3, wsrc2 = sw1, sw3, sw2
                for mat, hdst, fn_, bias_ in ((wsrc1, h1s, AF.Silu, cSB1),
                                              (wsrc3, h3s, AF.Copy, cSB3)):
                    for ic in range(8):
                        wb_ = wfb[ic % 2]
                        dma("sy", wb_[0].rearrange("p (c d) -> p c d", d=128),
                            mat[:, ic * 128:(ic + 1) * 128].rearrange("(c p) d -> p c d", p=128),
                            r=wb_[1], w=wb_[1])
                        pb2 = (4 if mat is wsrc3 else 0) + (ic % 2) * 2
                        for cc in range(8):
                            for tc in range(2):
                                mm(bank(pb2 + tc), wb_[0][:, cc * 128:(cc + 1) * 128],
                                   xnh[cc][0][:, tc * 512:(tc + 1) * 512],
                                   r=wb_[1] + (xnh[cc][1],), w=(PSB[pb2 + tc],),
                                   start=(cc == 0), stop=(cc == 7))
                        bias_ap = bias_[:, ic:ic + 1]
                        for tc in range(2):
                            act(h1s[ic][0][:, tc * 512:(tc + 1) * 512] if fn_ == AF.Silu
                                else h3s[ic][0][:, tc * 512:(tc + 1) * 512],
                                bank(pb2 + tc),
                                AF.Silu if fn_ == AF.Silu else AF.Identity,
                                r=(PSB[pb2 + tc], CB),
                                w=((h1s[ic][1],) if fn_ == AF.Silu else (h3s[ic][1],)),
                                bias=bias_ap)
                for ic in range(8):
                    ve_tt(h1s[ic][0], h1s[ic][0], h3s[ic][0], OP.mult,
                          r=(h1s[ic][1], h3s[ic][1]), w=(h1s[ic][1],))
                for dc in range(8):
                    wb_ = wfb[dc % 2]
                    dma("sy", wb_[0].rearrange("p (c d) -> p c d", d=128),
                        wsrc2[:, dc * 128:(dc + 1) * 128].rearrange("(c p) d -> p c d", p=128),
                        r=wb_[1], w=wb_[1])
                    pb2 = (dc % 2) * 2
                    for ic in range(8):
                        for tc in range(2):
                            mm(bank(pb2 + tc), wb_[0][:, ic * 128:(ic + 1) * 128],
                               h1s[ic][0][:, tc * 512:(tc + 1) * 512],
                               r=wb_[1] + (h1s[ic][1],), w=(PSB[pb2 + tc],),
                               start=(ic == 0), stop=(ic == 7))
                    for tc in range(2):
                        aslc = acc[dc][:, th * 1024 + tc * 512: th * 1024 + (tc + 1) * 512]
                        act(tmpz[0], bank(pb2 + tc), AF.Identity,
                            r=(PSB[pb2 + tc], CB), w=(tmpz[1],), bias=cSB2[:, dc:dc + 1])
                        ve_tt(aslc, aslc, tmpz[0], OP.add,
                              r=(accB[dc], tmpz[1]), w=(accB[dc],))
        P.barrier()

        outrow = [(tO[:, 0:1024], OB[0]), (tO[:, 1024:2048], OB[4])]
        xyrow = [(tO[:, 2048:3072], OB[1]), (tO[:, 3072:4096], OB[5])]
        xrow2 = [(tO[:, 4096:5120], OB[2]), (tO[:, 5120:6144], OB[6])]
        yfr2 = [(tO[:, 6144:7168], OB[3]), (tO[:, 7168:8192], OB[7])]
        for jp in range(NOWN // 2):
            for j in (2 * jp, 2 * jp + 1):
                sl = j % 2
                xr, xrB = xrow2[sl]
                yr, yrB = yfr2[sl]
                dma("gp", xr, x[2 * j * 128:(2 * j + 1) * 128, :], r=(xrB,), w=(xrB,))
                dma("gp", yr, yffn[j * 128:(j + 1) * 128, :], r=(YFB, yrB), w=(yrB,))
            for j in (2 * jp, 2 * jp + 1):
                sl = j % 2
                xy, xyB = xyrow[sl]
                xr, xrB = xrow2[sl]
                yr, yrB = yfr2[sl]
                ps = PS[j % 2]
                psb = (PSB[2 * (j % 2)], PSB[2 * (j % 2) + 1])
                ve_tt(xy, xr, yr, OP.add, r=(xrB, yrB, xyB), w=(xyB,))
                for dc in range(8):
                    tr(ps[:, dc * 128:(dc + 1) * 128], acc[dc][:, j * 128:(j + 1) * 128],
                       cID[:, :], r=(accB[dc], CB), w=(psb[dc // 4],), start=(dc % 4 == 0))
            for j in (2 * jp, 2 * jp + 1):
                sl = j % 2
                orow, orB = outrow[sl]
                xy, xyB = xyrow[sl]
                ps = PS[j % 2]
                psb = (PSB[2 * (j % 2)], PSB[2 * (j % 2) + 1])
                ve_tt(orow, xy, ps[:, :1024], OP.add, r=psb + (xyB, orB), w=(orB,))
                dma("gp", y[j * 128:(j + 1) * 128, :], orow, r=(orB,), w=(orB,))
        P.barrier()

        block = es.enter_context(nc.Block())
        eng_attr = {"te": "tensor", "ve": "vector", "se": "scalar",
                    "sy": "sync", "gp": "gpsimd"}
        for ename in ENGS:
            opl = P.ops[ename]
            if not opl:
                continue

            def body(eng, _opl=opl, _en=ename):
                for waits, fn, step, tag in _opl:
                    for dsem, v in waits:
                        eng.wait_ge(sem[dsem], v)
                    if fn is not None:
                        ins = fn(eng)
                        if step:
                            ins.then_inc(sem[tag], step)
            getattr(block, eng_attr[ename])(body)
    return nc


def _host_inputs(inp):
    x = np.asarray(inp["L0_attn_x"], dtype=np.float32)
    com = {
        "wqT": np.ascontiguousarray(np.asarray(inp["L0_attn_Wq"], np.float32).T),
        "wkT": np.ascontiguousarray(np.asarray(inp["L0_attn_Wk"], np.float32).T),
        "wvT": np.ascontiguousarray(np.asarray(inp["L0_attn_Wv"], np.float32).T),
        "woT": np.ascontiguousarray(np.asarray(inp["L0_attn_Wo"], np.float32).T),
        "gwT": np.ascontiguousarray(np.asarray(inp["L0_ffn_gate_w"], np.float32).T),
        "w1": np.ascontiguousarray(np.asarray(inp["L0_ffn_w1"], np.float32)),
        "w2": np.ascontiguousarray(np.asarray(inp["L0_ffn_w2"], np.float32)),
        "w3": np.ascontiguousarray(np.asarray(inp["L0_ffn_w3"], np.float32)),
        "b2": np.ascontiguousarray(np.asarray(inp["L0_ffn_b2"], np.float32)),
        "sw1": np.ascontiguousarray(np.asarray(inp["L0_ffn_sw1"], np.float32)),
        "sw2": np.ascontiguousarray(np.asarray(inp["L0_ffn_sw2"], np.float32)),
        "sw3": np.ascontiguousarray(np.asarray(inp["L0_ffn_sw3"], np.float32)),
    }
    for nm, key in (("b1c", "L0_ffn_b1"), ("b3c", "L0_ffn_b3")):
        bmat = np.asarray(inp[key], np.float32)
        com[nm] = np.ascontiguousarray(
            bmat.reshape(8, 8, 128).transpose(2, 0, 1).reshape(128, 64))
    for nm, key in (("sb1c", "L0_ffn_sb1"), ("sb2c", "L0_ffn_sb2"), ("sb3c", "L0_ffn_sb3")):
        v = np.asarray(inp[key], np.float32)
        com[nm] = np.ascontiguousarray(v.reshape(8, 128).T)
    com["ident"] = np.eye(128, dtype=np.float32)
    ctri = np.zeros((128, 144), dtype=np.float32)
    ctri[:, 0:128] = (np.arange(128)[:, None] <= np.arange(128)[None, :])
    ctri[0:16, 128:144] = (np.arange(16)[:, None] < np.arange(16)[None, :])
    com["ctri"] = ctri
    com["ci128"] = np.arange(128, dtype=np.float32).reshape(128, 1)
    con2 = np.zeros((128, 2), dtype=np.float32)
    con2[0:64, 0] = 1.0
    con2[64:128, 1] = 1.0
    com["cones2"] = con2
    csl2 = np.zeros((2, 256), dtype=np.float32)
    csl2[0, 0:128] = 1.0
    csl2[1, 128:256] = 1.0
    com["csel2"] = csl2
    cslp = np.zeros((2, NEXP), dtype=np.float32)
    for p_ in range(NEXP):
        cslp[0, p_] = SLOPES[2 * p_]
        cslp[1, p_] = SLOPES[2 * p_ + 1]
    com["cslopes"] = cslp
    com["bco"] = np.ones((1, 128), dtype=np.float32)
    com["onescol"] = np.ones((128, 1), dtype=np.float32)
    tri = np.where(np.arange(128)[:, None] > np.arange(128)[None, :], BIG, 0.0).astype(np.float32)
    in_maps = []
    for core in range(8):
        b, par = core // 2, core % 2
        m = dict(com)
        xb = x[b].reshape(NBLK, 128, C)
        if par == 1:
            perm = np.arange(NBLK).reshape(NBLK // 2, 2)[:, ::-1].reshape(NBLK)
            xb = xb[perm]
        m["x"] = np.ascontiguousarray(xb.reshape(T, C))
        tb = np.arange(NBLK) ^ par
        ali = np.zeros((128, 512), dtype=np.float32)
        for h in range(H):
            for sc in range(NBLK):
                ali[:, h * 32 + sc] = SLOPES[h] * (tb[sc] * 128 + np.arange(128))
        m["alibi"] = ali
        m["maska"] = tri
        m["maskb"] = (np.full((128, 128), BIG, dtype=np.float32) if par == 0
                      else np.zeros((128, 128), dtype=np.float32))
        iota = np.zeros((1, TOWN), dtype=np.float32)
        for j in range(NOWN):
            iota[0, j * 128:(j + 1) * 128] = (2 * j + par) * 128 + np.arange(128)
        m["iota"] = iota
        in_maps.append(m)
    return in_maps


def kernel(**inp):
    in_maps = _host_inputs(inp)
    nc = _build()
    res = run_bass_kernel_spmd(nc, in_maps, core_ids=list(range(8)))
    out = np.zeros((B, T, C), dtype=np.float32)
    for core in range(8):
        b, par = core // 2, core % 2
        yc = np.asarray(res.results[core]["y"]).reshape(NOWN, 128, C)
        for j in range(NOWN):
            out[b, (2 * j + par) * 128:(2 * j + par + 1) * 128, :] = yc[j]
    return out


# version 36
# speedup vs baseline: 1.0745x; 1.0091x over previous
import contextlib
import math

import numpy as np

import concourse.bass as bass
import concourse.mybir as mybir
from concourse.bass_utils import run_bass_kernel_spmd
from concourse.alu_op_type import AluOpType as OP

F32 = mybir.dt.float32
AF = mybir.ActivationFunctionType
AX = mybir.AxisListType

B, T, C = 4, 4096, 1024
H, DH = 16, 64
NBLK = 32
NOWN = 16
TOWN = NOWN * 128
NCC = 8
NEXP = 8
CCAP = 384
SLOPES = [2.0 ** (-(h + 1) / 2.0) for h in range(H)]
BIG = 1e30

ENGS = ("te", "ve", "se", "sy", "gp")
STREAMS = {"sW": "sy", "sC": "sy", "gX": "gp", "gI": "gp", "gS": "gp"}
SEMS = ENGS + tuple(STREAMS)


class _Buf:
    __slots__ = ("w", "readers")

    def __init__(self):
        self.w = None
        self.readers = {}


class _Prog:
    def __init__(self):
        self.ops = {e: [] for e in ENGS}
        self.cnt = {s: 0 for s in SEMS}
        self.waited = {e: {} for e in ENGS}

    SELF_MARGIN = 6

    def emit(self, eng, fn, r=(), w=(), step=1, stream=None, war_only=False):
        tag = stream or eng
        waits = {}

        def need(dep, val):
            if dep == eng and dep in ("te", "ve", "se"):
                if self.cnt[eng] - val >= self.SELF_MARGIN:
                    return
            if dep in STREAMS:
                val = self.cnt[dep]
            if self.waited[eng].get(dep, 0) >= val:
                return
            if waits.get(dep, 0) >= val:
                return
            assert val <= self.cnt[dep], (eng, dep, val, self.cnt[dep])
            waits[dep] = val

        for buf in r:
            if buf.w is not None:
                need(*buf.w)
        for buf in w:
            if buf.w is not None and not war_only:
                need(*buf.w)
            for de, dv in buf.readers.items():
                need(de, dv)
        self.cnt[tag] += step
        after = self.cnt[tag]
        for de, dv in waits.items():
            self.waited[eng][de] = max(self.waited[eng].get(de, 0), dv)
        self.ops[eng].append((sorted(waits.items()), fn, step, tag))
        for buf in r:
            buf.readers[tag] = after
        for buf in w:
            buf.w = (tag, after)
            buf.readers = {}

    def barrier(self):
        snap = dict(self.cnt)
        for eng in ENGS:
            waits = []
            for de in SEMS:
                if de == eng and de in ("te", "ve", "se"):
                    continue
                v = snap[de]
                if v and self.waited[eng].get(de, 0) < v:
                    waits.append((de, v))
                    self.waited[eng][de] = v
            if waits:
                self.ops[eng].append((waits, None, 0, eng))


def _build(taps=()):
    nc = bass.Bass("TRN2", target_bir_lowering=False)

    def din(name, shape):
        return nc.dram_tensor(name, shape, F32, kind="ExternalInput").ap()

    x = din("x", [T, C])
    wqT = din("wqT", [C, C])
    wkT = din("wkT", [C, C])
    wvT = din("wvT", [C, C])
    woT = din("woT", [C, C])
    gwT = din("gwT", [C, NEXP])
    w1 = din("w1", [NEXP, C, C])
    w2 = din("w2", [NEXP, C, C])
    w3 = din("w3", [NEXP, C, C])
    b2 = din("b2", [NEXP, C])
    b1c = din("b1c", [128, 64])
    b3c = din("b3c", [128, 64])
    sw1 = din("sw1", [C, C])
    sw2 = din("sw2", [C, C])
    sw3 = din("sw3", [C, C])
    sb1c = din("sb1c", [128, 8])
    sb2c = din("sb2c", [128, 8])
    sb3c = din("sb3c", [128, 8])
    ident = din("ident", [128, 128])
    bco = din("bco", [1, 128])
    onescol = din("onescol", [128, 1])
    alibi_in = din("alibi", [128, 512])
    maska_in = din("maska", [128, 128])
    maskb_in = din("maskb", [128, 128])
    iota_in = din("iota", [1, TOWN])
    ctri_in = din("ctri", [128, 144])
    ci128_in = din("ci128", [128, 1])
    cones2_in = din("cones2", [128, 2])
    cslopes_in = din("cslopes", [2, NEXP])
    csel2_in = din("csel2", [2, 256])

    y = nc.dram_tensor("y", [TOWN, C], F32, kind="ExternalOutput").ap()
    xnf = nc.dram_tensor("xnf", [C, T], F32,
                         kind="ExternalOutput" if "xnf" in taps else "Internal").ap()
    xntd = nc.dram_tensor("xnt", [TOWN, C], F32).ap()
    combd = nc.dram_tensor("combd", [TOWN + 512, NEXP], F32,
                           kind="ExternalOutput" if "combd" in taps else "Internal").ap()
    listsd = nc.dram_tensor("listsd", [2 * NEXP * 640, 1], mybir.dt.uint32,
                            kind="ExternalOutput" if "listsd" in taps else "Internal").ap()
    yffn = nc.dram_tensor("yffn", [TOWN + CCAP + 8, C], F32,
                          kind="ExternalOutput" if "yffn" in taps else "Internal").ap()
    tapd = {}
    for tname, shape in (("kph0", [128, T]), ("kpair", [128, T]), ("qpair", [128, TOWN]),
                         ("vtiles", [128, 4160]), ("oall", [C, TOWN]),
                         ("accp3", [C, TOWN]), ("probe", [128, 512]),
                         ("comb", [8, TOWN])):
        if tname in taps:
            tapd[tname] = nc.dram_tensor("tap_" + tname, shape, F32,
                                         kind="ExternalOutput").ap()

    P = _Prog()
    es = contextlib.ExitStack()
    with es:
        sem = {e: es.enter_context(nc.semaphore("s_" + e)) for e in SEMS}

        def sb(name, shape):
            return es.enter_context(nc.sbuf_tensor(name, shape, F32))

        tO = sb("tO", [128, 16384])
        tXA = sb("tXA", [128, 16384])
        tK = sb("tK", [128, 4096])
        tV = sb("tV", [128, 4160])
        tQ = sb("tQ", [128, 2048])
        tP = sb("tP", [128, 2048])
        tSH = sb("tSH", [128, 1024])
        tW = sb("tW", [128, 3072])
        tM = sb("tM", [128, 192])
        cID = sb("cID", [128, 128])
        cBCO = sb("cBCO", [1, 128])
        cONE = sb("cONE", [128, 1])
        cALI = sb("cALI", [128, 512])
        cMA = sb("cMA", [128, 128])
        cMB = sb("cMB", [128, 128])
        cB1 = sb("cB1", [128, 64])
        cB3 = sb("cB3", [128, 64])
        cSB1 = sb("cSB1", [128, 8])
        cSB2 = sb("cSB2", [128, 8])
        cSB3 = sb("cSB3", [128, 8])
        cGW = sb("cGW", [128, 64])
        cTRI = sb("cTRI", [128, 144])
        cI128 = sb("cI128", [128, 1])
        cON2 = sb("cON2", [128, 2])
        cSLP = sb("cSLP", [2, NEXP])
        cSL2 = sb("cSL2", [2, 256])
        cSENT = es.enter_context(nc.sbuf_tensor("cSENT", [128, 5], mybir.dt.uint32))

        PS = [es.enter_context(nc.psum_tensor("ps%d" % i, [128, 1024], F32))
              for i in range(4)]

        OB = [_Buf() for _ in range(8)]
        XB = [_Buf() for _ in range(4)]
        KB = [_Buf() for _ in range(4)]
        VB = [_Buf() for _ in range(4)]
        QB = [_Buf() for _ in range(2)]
        PB = [_Buf() for _ in range(4)]
        SHB = [_Buf() for _ in range(2)]
        WB = [_Buf() for _ in range(3)]
        MB = _Buf()
        CB = _Buf()
        PSB = [_Buf() for _ in range(8)]
        XNF = _Buf()
        XNT = _Buf()
        CMD = _Buf()
        LB = [_Buf() for _ in range(NEXP)]
        PFH = [[_Buf() for _ in range(NEXP)] for _ in range(2)]
        YFB = _Buf()

        def bank(i):
            return PS[i // 2][:, (i % 2) * 512:(i % 2 + 1) * 512]

        def ob(c0, c1):
            return tuple(OB[c0 // 2048:(c1 - 1) // 2048 + 1])

        def dma(q, dst, src, r, w, stream=None, war_only=False):
            if stream is None:
                stream = "sW" if q == "sy" else "gX"
            P.emit(q, lambda e, a=dst, b=src: e.dma_start(out=a, in_=b),
                   r=r, w=w, step=16, stream=stream, war_only=war_only)

        def idma_gather(out, in_, idx, r, w):
            P.emit("gp", lambda e, o=out, i=in_, x=idx:
                   e.indirect_dma_start(out=o, out_offset=None, in_=i,
                                        in_offset=bass.IndirectOffsetOnAxis(ap=x, axis=0)),
                   r=r, w=w, step=16, stream="gI")

        def idma_scatter(out, in_, idx, r, w, accum=False, war_only=False):
            kw = {"compute_op": OP.add} if accum else {}
            P.emit("gp", lambda e, o=out, i=in_, x=idx, k=kw:
                   e.indirect_dma_start(out=o, out_offset=bass.IndirectOffsetOnAxis(ap=x, axis=0),
                                        in_=i, in_offset=None, **k),
                   r=r, w=w, step=16, stream="gI", war_only=war_only)

        def mm(out, lhsT, rhs, r, w, start=True, stop=True):
            P.emit("te",
                   lambda e, o=out, l=lhsT, m=rhs, st=start, sp=stop:
                   e.matmul(o, l, m, start=st, stop=sp, skip_group_check=True),
                   r=r, w=w)

        def tr(out, in_, idn, r, w, start=True):
            P.emit("te", lambda e, o=out, i=in_, d=idn, s=start:
                   e.matmul(o, i, d, start=s, stop=True, is_transpose=True,
                            skip_group_check=True), r=r, w=w)

        def ve_copy(out, in_, r, w):
            P.emit("ve", lambda e, o=out, i=in_: e.tensor_copy(o, i), r=r, w=w)

        def ve_tt(out, in0, in1, op, r, w):
            P.emit("ve", lambda e, o=out, a=in0, b=in1, p=op: e.tensor_tensor(o, a, b, p),
                   r=r, w=w)

        def ve_ts(out, in0, s1, op0, r, w, s2=None, op1=None):
            if op1 is None:
                P.emit("ve", lambda e, o=out, a=in0, s=s1, p=op0:
                       e.tensor_scalar(o, a, s, None, p), r=r, w=w)
            else:
                P.emit("ve", lambda e, o=out, a=in0, s=s1, t=s2, p=op0, q=op1:
                       e.tensor_scalar(o, a, s, t, p, q), r=r, w=w)

        def ve_stt(out, in0, s, in1, op0, op1, r, w):
            P.emit("ve", lambda e, o=out, a=in0, c=s, b=in1, p=op0, q=op1:
                   e.scalar_tensor_tensor(o, a, c, b, p, q), r=r, w=w)

        def ve_red(out, in_, op, r, w, absv=None, neg=None):
            if op == "max":
                P.emit("ve", lambda e, o=out, i=in_, a=absv, n=neg:
                       e.reduce_max(o, i, axis=AX.X, apply_absolute_value=a, negate=n),
                       r=r, w=w)
            else:
                P.emit("ve", lambda e, o=out, i=in_: e.reduce_sum(o, i, axis=AX.X),
                       r=r, w=w)

        def ve_recip(out, in_, r, w):
            P.emit("ve", lambda e, o=out, i=in_: e.reciprocal(o, i), r=r, w=w)

        def ve_memset(ap, const, w):
            P.emit("ve", lambda e, a=ap, c=const: e.memset(a, c), r=(), w=w)

        def act(out, in_, func, r, w, bias=None, scale=None):
            kw = {}
            if bias is not None:
                kw["bias"] = bias
            if scale is not None:
                kw["scale"] = scale
            P.emit("se", lambda e, o=out, i=in_, f=func, k=kw: e.activation(o, i, f, **k),
                   r=r, w=w)

        for dst, src in ((cID, ident), (cBCO, bco), (cONE, onescol), (cALI, alibi_in),
                         (cMA, maska_in), (cMB, maskb_in),
                         (cB1, b1c), (cB3, b3c), (cSB1, sb1c), (cSB2, sb2c),
                         (cSB3, sb3c), (cTRI, ctri_in), (cI128, ci128_in),
                         (cON2, cones2_in), (cSL2, csel2_in), (cSLP, cslopes_in)):
            dma("sy", dst[:, :], src, r=(), w=(CB,), stream="sC")
        dma("sy", cGW[:, :].rearrange("p (c g) -> p c g", g=8),
            gwT.rearrange("(c p) g -> p c g", p=128), r=(), w=(CB,), stream="sC")

        padf = tM[:, 160:161]
        P.emit("ve", lambda e: e.tensor_scalar(padf, cI128[:, :], float(CCAP + 1), None, OP.add),
               r=(CB,), w=(MB,))
        for c5 in range(5):
            P.emit("ve", lambda e, c=c5: e.tensor_scalar(tM[:, 161 + c:162 + c], cI128[:, :],
                                                         5.0, float(TOWN + c), OP.mult, OP.add),
                   r=(CB,), w=(MB,))
        ve_copy(cSENT[:, :], tM[:, 161:166], r=(MB,), w=(CB,))
        zb = tO[:, 0:1024]
        ve_memset(zb, 0.0, w=(OB[0],))
        for zi in range((TOWN + CCAP + 8) // 128 + 1):
            r0 = zi * 128
            r1 = min(r0 + 128, TOWN + CCAP + 8)
            if r0 >= r1:
                break
            dma("sy", yffn[r0:r1, :], zb[0:r1 - r0, :], r=(OB[0],), w=(YFB,), stream="sC")
        for zi in range(4):
            dma("sy", combd[TOWN + zi * 128: TOWN + (zi + 1) * 128, :], zb[:, 0:8],
                r=(OB[0],), w=(CMD,), stream="sC")

        x4 = [(tK[:, 0:4096], tuple(KB)), (tV[:, 0:4096], tuple(VB))]
        sq4 = (tO[:, 10240:14336], ob(10240, 14336))
        stage = tO[:, 2048:10240]
        stB = ob(2048, 10240)
        for grp in range(NBLK // 4):
            sl = grp % 2
            xb, xbB = x4[sl]
            sq_, sqB = sq4
            for k in range(4):
                tt = grp * 4 + k
                dma("gp", xb[:, k * 1024:(k + 1) * 1024], x[tt * 128:(tt + 1) * 128, :],
                    r=xbB, w=xbB)
            s4 = tM[:, 20 + sl * 16: 24 + sl * 16]
            mean4 = tM[:, 24 + sl * 16: 28 + sl * 16]
            var4 = tM[:, 28 + sl * 16: 32 + sl * 16]
            rst4 = tM[:, 32 + sl * 16: 36 + sl * 16]

            def s4v(ap):
                return ap.rearrange("p (t o) -> p t o", o=1)

            def x4v(ap):
                return ap.rearrange("p (t c) -> p t c", c=C)

            P.emit("ve", lambda e, a=s4v(s4), b=x4v(xb): e.tensor_reduce(a, b, axis=AX.X,
                                                                         op=OP.add),
                   r=xbB, w=(MB,))
            ve_ts(mean4, s4, 1.0 / C, OP.mult, r=(MB,), w=(MB,))
            P.emit("ve", lambda e, a=x4v(xb), m=s4v(mean4).broadcast_to((128, 4, C)):
                   e.tensor_tensor(a, a, m, OP.subtract), r=xbB + (MB,), w=xbB)
            act(sq_, xb, AF.Square, r=xbB + sqB, w=sqB)
            P.emit("ve", lambda e, a=s4v(s4), b=x4v(sq_): e.tensor_reduce(a, b, axis=AX.X,
                                                                          op=OP.add),
                   r=sqB, w=(MB,))
            ve_ts(var4, s4, 1.0 / C, OP.mult, r=(MB,), w=(MB,), s2=1e-5, op1=OP.add)
            act(var4, var4, AF.Sqrt, r=(MB,), w=(MB,))
            ve_recip(rst4, var4, r=(MB,), w=(MB,))
            P.emit("ve", lambda e, a=x4v(xb), m=s4v(rst4).broadcast_to((128, 4, C)):
                   e.tensor_tensor(a, a, m, OP.mult), r=xbB + (MB,), w=xbB)
            for k in range(4):
                tt = grp * 4 + k
                xn_ = xb[:, k * 1024:(k + 1) * 1024]
                ps = PS[tt % 2]
                psb = (PSB[2 * (tt % 2)], PSB[2 * (tt % 2) + 1])
                for cc in range(NCC):
                    tr(ps[:, cc * 128:(cc + 1) * 128], xn_[:, cc * 128:(cc + 1) * 128],
                       cID[:, :], r=xbB + (CB,), w=(psb[cc // 4],), start=(cc % 4 == 0))
                k8 = tt % 8
                stv = stage.rearrange("p (c t) -> p c t", t=1024)[:, :, k8 * 128:(k8 + 1) * 128]
                P.emit("ve", lambda e, o=stv, i=ps[:, :1024].rearrange("p (c t) -> p c t", t=128):
                       e.tensor_copy(o, i), r=psb + stB, w=stB)
                if tt % 2 == 0:
                    jown = tt // 2
                    dma("gp", xntd[jown * 128:(jown + 1) * 128, :], xn_, r=xbB, w=(XNT,),
                        stream="gS", war_only=True)
            if grp % 2 == 1:
                g8 = grp // 2
                for cc in range(NCC):
                    dma("gp", xnf[cc * 128:(cc + 1) * 128, g8 * 1024:(g8 + 1) * 1024],
                        stage[:, cc * 1024:(cc + 1) * 1024], r=stB, w=(XNF,),
                        stream="gS", war_only=True)
        P.barrier()

        for sc in range(32):
            ve_memset(tV[:, sc * 130 + 64: sc * 130 + 65], 1.0, w=(VB[(sc * 130) // 1040],))
            ve_memset(tV[:, sc * 130 + 129: sc * 130 + 130], 1.0, w=(VB[(sc * 130 + 129) // 1040],))

        kpair = tK[:, :]
        kpB = KB
        qpair = tQ[:, :]
        qpB = QB

        def vtile(sc):
            return tV[:, sc * 130:(sc + 1) * 130]

        def vtB(sc):
            return VB[(sc * 130) // 1040]

        xach = [tXA[:, i * 4096:(i + 1) * 4096] for i in range(4)]
        sqa = tXA[:, 0:4096]
        normq2 = tXA[0:1, 4096:6144]
        shiftrow = [tXA[0:1, 6144:8192], tXA[0:1, 8192:10240]]
        normk2 = tXA[0:1, 10240:14336]
        iotarow = tXA[0:1, 10240:12288]
        sloperow = tXA[0:1, 12288:14336]
        maxk2 = tM[0:1, 130:131]
        maxk8 = tM[0:1, 131:132]
        opair = tM[:, 0:128]
        recc = [tM[:, 128:129], tM[:, 129:130]]
        wqb = (tW[:, 0:1024], WB[0])
        wkb = (tW[:, 1024:2048], WB[1])
        wvb = (tW[:, 2048:3072], WB[2])
        pbuf = [[(tP[:, 0:512], PB[0]), (tP[:, 512:1024], PB[1])],
                [(tP[:, 1024:1536], PB[2]), (tP[:, 1536:2048], PB[3])]]
        shbc = [(tSH[:, 0:512], SHB[0]), (tSH[:, 512:1024], SHB[1])]

        def emit_wload(pr):
            for wb_, wsrc in ((wqb, wqT), (wkb, wkT), (wvb, wvT)):
                dma("sy", wb_[0].rearrange("p (c d) -> p c d", d=128),
                    wsrc[:, pr * 128:(pr + 1) * 128].rearrange("(c p) d -> p c d", p=128),
                    r=(wb_[1],), w=(wb_[1],))

        emit_wload(0)
        for pair in range(8):
            for hidx2, half in enumerate(range(2)):
                iord = (2, 3, 0, 1) if half == 0 else (0, 1, 2, 3)
                for i in range(4):
                    cc = half * 4 + i
                    if half == 0 and pair > 0 and i >= 2:
                        continue
                    dma("gp", xach[i], xnf[cc * 128:(cc + 1) * 128, :], r=(XB[i], XNF), w=(XB[i],))
                for ii, i in enumerate(iord):
                    for scp in range(8):
                        mm(bank(scp), wkb[0][:, (half * 4 + i) * 128:(half * 4 + i + 1) * 128],
                           xach[i][:, scp * 512:(scp + 1) * 512],
                           r=(wkb[1], XB[i]), w=(PSB[scp],), start=(ii == 0), stop=(ii == 3))
                for scp in range(8):
                    dst = kpair[:, scp * 512:(scp + 1) * 512]
                    bref = kpB[scp // 2]
                    if half == 0:
                        ve_copy(dst, bank(scp), r=(PSB[scp],), w=(bref,))
                    else:
                        ve_tt(dst, dst, bank(scp), OP.add, r=(PSB[scp], bref), w=(bref,))
                if "kph0" in tapd and pair == 0 and half == 0:
                    dma("gp", tapd["kph0"], kpair, r=tuple(kpB), w=())
                for ii, i in enumerate(iord):
                    for scp in range(8):
                        mm(bank(scp), wvb[0][:, (half * 4 + i) * 128:(half * 4 + i + 1) * 128],
                           xach[i][:, scp * 512:(scp + 1) * 512],
                           r=(wvb[1], XB[i]), w=(PSB[scp],), start=(ii == 0), stop=(ii == 3))
                for scp in range(8):
                    dst = tV[:, scp * 512:(scp + 1) * 512]
                    if hidx2 == 0:
                        ve_copy(dst, bank(scp), r=(PSB[scp],) + tuple(VB), w=tuple(VB))
                    else:
                        ve_tt(dst, dst, bank(scp), OP.add, r=(PSB[scp],) + tuple(VB),
                              w=tuple(VB))
                for ii, i in enumerate(iord):
                    for qt in range(4):
                        rhs = xach[i].rearrange("p (b c) -> p b c", c=128)[:, 8 * qt: 8 * qt + 8: 2, :]
                        mm(bank(qt), wqb[0][:, (half * 4 + i) * 128:(half * 4 + i + 1) * 128],
                           rhs, r=(wqb[1], XB[i]), w=(PSB[qt],), start=(ii == 0), stop=(ii == 3))
                for qt in range(4):
                    dst = qpair[:, qt * 512:(qt + 1) * 512]
                    bref = qpB[qt // 2]
                    if half == 0:
                        ve_copy(dst, bank(qt), r=(PSB[qt],), w=(bref,))
                    else:
                        ve_tt(dst, dst, bank(qt), OP.add, r=(PSB[qt], bref), w=(bref,))

            for sc in range(31, -1, -1):
                bk = sc % 2
                tr(bank(bk)[:, 0:128], tV[:, sc * 128:(sc + 1) * 128], cID[:, :],
                   r=tuple(VB) + (CB,), w=(PSB[bk],))
                vt = vtile(sc)
                dst = vt[:, 0:130].rearrange("p (g d) -> p g d", d=65)[:, :, 0:64]
                srcv = bank(bk)[:, 0:128].rearrange("p (g d) -> p g d", d=64)
                ve_copy(dst, srcv, r=(PSB[bk],) + tuple(VB), w=tuple(VB))
            for oc in (64, 129):
                ve_memset(tV[:, 0:4160].rearrange("p (s c) -> p s c", c=130)[:, :, oc:oc + 1],
                          1.0, w=tuple(VB))

            if "probe" in tapd and pair == 0:
                dma("gp", tapd["probe"], sqa[:, 0:512], r=(XB[0],), w=())
            if "kpair" in tapd and pair == 0:
                dma("gp", tapd["kpair"], kpair, r=tuple(kpB), w=())
            if "qpair" in tapd and pair == 0:
                dma("gp", tapd["qpair"], qpair, r=tuple(qpB), w=())
            if "vtiles" in tapd and pair == 0:
                dma("gp", tapd["vtiles"], tV[:, :], r=tuple(VB), w=())

            normq2f = tXA[0:2, 4096:6144]
            srow2 = tXA[0:2, 6144:8192]
            iota2 = tXA[0:2, 8192:10240]
            normk2f = tXA[0:2, 10240:14336]
            slope2 = tXA[0:2, 12288:14336]
            maxk2b = tM[0:2, 130:131]
            maxk8b = tM[0:2, 131:132]
            act(sqa[:, 0:4096], kpair[:, :], AF.Square, r=tuple(kpB), w=(XB[0],))
            for ch in range(8):
                mm(bank(ch)[0:2, :], cON2[:, :], sqa[:, ch * 512:(ch + 1) * 512],
                   r=(CB, XB[0]), w=(PSB[ch],))
            for i4 in range(4):
                ve_copy(normk2f[:, i4 * 1024:(i4 + 1) * 1024], PS[i4][0:2, :],
                        r=(PSB[2 * i4], PSB[2 * i4 + 1]), w=(XB[2], XB[3]))
            ve_red(maxk2b, normk2f, "max", r=(XB[2], XB[3]), w=(MB,))
            act(maxk8b, maxk2b, AF.Sqrt, r=(MB,), w=(MB,), scale=1.0 / 64.0)
            act(sqa[:, 0:2048], qpair[:, :], AF.Square, r=tuple(qpB) + (XB[0],), w=(XB[0],))
            for ch in range(4):
                mm(bank(ch)[0:2, :], cON2[:, :], sqa[:, ch * 512:(ch + 1) * 512],
                   r=(CB, XB[0]), w=(PSB[ch],))
            for i4 in range(2):
                ve_copy(normq2f[:, i4 * 1024:(i4 + 1) * 1024], PS[i4][0:2, :],
                        r=(PSB[2 * i4], PSB[2 * i4 + 1]), w=(XB[1],))
            ve_ts(srow2, normq2f, 1e-30, OP.add, r=(XB[1],), w=(XB[1],))
            act(srow2, srow2, AF.Sqrt, r=(XB[1],), w=(XB[1],))
            for r2 in range(2):
                dma("gp", iota2[r2:r2 + 1, :], iota_in, r=(XB[2],), w=(XB[2],))
            ve_tt(slope2, iota2, cSLP[:, pair:pair + 1].broadcast_to((2, TOWN)),
                  OP.mult, r=(XB[2], XB[3], CB), w=(XB[3],))
            ve_tt(srow2, srow2, maxk8b.broadcast_to((2, TOWN)), OP.mult,
                  r=(XB[1], MB), w=(XB[1],))
            ve_tt(srow2, srow2, slope2, OP.add, r=(XB[1], XB[3]), w=(XB[1],))
            if pair < 7:
                for i in (2, 3):
                    dma("gp", xach[i], xnf[i * 128:(i + 1) * 128, :],
                        r=(XB[i], XNF), w=(XB[i],), stream="gS")
                emit_wload(pair + 1)

            for qt in range(4):
                for hh in range(2):
                    mm(bank(6), cSL2[:, hh * 128:(hh + 1) * 128],
                       srow2[:, qt * 512:(qt + 1) * 512],
                       r=(CB, XB[1]), w=(PSB[6],))
                    ve_copy(shbc[hh][0], bank(6), r=(PSB[6],), w=(shbc[hh][1],))
                scmax = 8 * qt + 7
                av_bank = (PSB[4], PSB[5])
                pend = None

                def emit_av(sc, js):
                    for hh in range(2):
                        pb, pbB = pbuf[hh][sc % 2]
                        mm(bank(4 + hh)[0:65, js * 128:512],
                           vtile(sc)[:, hh * 65:(hh + 1) * 65],
                           pb[:, js * 128:512],
                           r=(pbB, vtB(sc)), w=(av_bank[hh],),
                           start=(sc == 0), stop=(sc == scmax))

                for sc in range(scmax + 1):
                    js = min(j for j in range(4) if sc <= 2 * (4 * qt + j) + 1)
                    w_ = 512 - js * 128
                    for hh in range(2):
                        h = 2 * pair + hh
                        rows = slice(hh * 64, (hh + 1) * 64)
                        bkid = (sc % 3) if hh == 0 else (3, 6, 7)[sc % 3]
                        sB = PSB[bkid]
                        sap = bank(bkid)
                        mm(sap[:, js * 128: 512], kpair[rows, sc * 128:(sc + 1) * 128],
                           qpair[rows, qt * 512 + js * 128:(qt + 1) * 512],
                           r=tuple(kpB) + tuple(qpB), w=(sB,))
                        ve_stt(sap[:, js * 128:512], sap[:, js * 128:512], 0.125,
                               shbc[hh][0][:, js * 128:512], OP.mult, OP.subtract,
                               r=(sB, shbc[hh][1]), w=(sB,))
                        for j in range(js, 4):
                            d = sc - 2 * (4 * qt + j)
                            if d == 0:
                                mk = cMA
                            elif d == 1:
                                mk = cMB
                            else:
                                continue
                            ve_tt(sap[:, j * 128:(j + 1) * 128], sap[:, j * 128:(j + 1) * 128],
                                  mk[:, :], OP.subtract, r=(sB, CB), w=(sB,))
                        pb, pbB = pbuf[hh][sc % 2]
                        act(pb[:, js * 128:512], sap[:, js * 128:512], AF.Exp,
                            r=(sB, CB), w=(pbB,), bias=cALI[:, h * 32 + sc: h * 32 + sc + 1])
                    if pend is not None:
                        emit_av(*pend)
                    pend = (sc, js)
                emit_av(*pend)

                ocols = slice(pair * 2048 + qt * 512, pair * 2048 + (qt + 1) * 512)
                dn2 = tXA[0:1, 0:4096]
                for hh in range(2):
                    dslc = dn2[0:1, hh * 2048 + qt * 512: hh * 2048 + (qt + 1) * 512]
                    ve_copy(dslc, bank(4 + hh)[64:65, :], r=(av_bank[hh],), w=(XB[0],))
                    ve_recip(dslc, dslc, r=(XB[0],), w=(XB[0],))
                    if hh == 0:
                        ve_copy(tO[0:64, ocols], bank(4)[0:64, :],
                                r=(av_bank[0],), w=(OB[pair],))
                    else:
                        stg = tP[0:64, 0:512]
                        ve_copy(stg, bank(5)[0:64, :], r=(av_bank[1],), w=(PB[0],))
                        dma("gp", tO[64:128, ocols], stg, r=(PB[0],), w=(OB[pair],))

            dn2 = tXA[0:1, 0:4096]
            for q4 in range(4):
                for hh in range(2):
                    oc = slice(pair * 2048 + q4 * 512, pair * 2048 + (q4 + 1) * 512)
                    bk = 4 + hh
                    mm(bank(bk), cBCO[0:1, :],
                       dn2[0:1, hh * 2048 + q4 * 512: hh * 2048 + (q4 + 1) * 512],
                       r=(CB, XB[0]), w=(PSB[bk],))
                    if hh == 0:
                        ve_tt(tO[0:64, oc], tO[0:64, oc], bank(4)[0:64, :], OP.mult,
                              r=(OB[pair], PSB[4]), w=(OB[pair],))
                    else:
                        ve_tt(tO[64:128, oc], tO[64:128, oc], bank(5)[64:128, :], OP.mult,
                              r=(OB[pair], PSB[5]), w=(OB[pair],))
        if "oall" in tapd:
            for cc in range(8):
                dma("gp", tapd["oall"][cc * 128:(cc + 1) * 128, :],
                    tO[:, cc * 2048:(cc + 1) * 2048], r=(OB[cc],), w=())
        P.barrier()

        acc = [tXA[:, dc * 2048:(dc + 1) * 2048] for dc in range(8)]
        accB = [XB[dc // 2] for dc in range(8)]
        h1s = [(tO[:, ic * 1024:(ic + 1) * 1024], OB[ic // 2]) for ic in range(8)]
        h3s = [(tO[:, 8192 + ic * 1024: 8192 + (ic + 1) * 1024], OB[4 + ic // 2]) for ic in range(8)]
        wfb = [(tP[:, 0:1024], (PB[0], PB[1])), (tW[:, 0:1024], (WB[0],))]
        gwrk = tP[:, 1024:2048]
        gwB = (PB[2], PB[3])
        tmpz = (tQ[:, 1024:1536], QB[1])
        cB2 = tW[0:8, 2048:3072]
        dma("sy", cB2, b2, r=(WB[2],), w=(WB[2],))
        combT2 = [(tSH[0:8, 0:1024], SHB[0]), (tW[0:8, 1024:2048], WB[1])]
        logi = (tQ[0:8, 0:1024], QB[0])
        cums = (tQ[0:8, 1024:2048], QB[1])
        h1sp = [tO[:, ic * CCAP:(ic + 1) * CCAP] for ic in range(8)]
        h3sp = [tO[:, 3072 + ic * CCAP: 3072 + (ic + 1) * CCAP] for ic in range(8)]
        xgt = [tO[:, 6144 + i * 1024: 6144 + (i + 1) * 1024] for i in range(3)]
        xgf = [tO[:, 9216 + cc * CCAP: 9216 + (cc + 1) * CCAP] for cc in range(8)]
        ztok = [tO[:, 12288 + i * 1024: 12288 + (i + 1) * 1024] for i in range(3)]

        ltok = tP[:, 1024:1088]
        pshift = tP[:, 1088:1152]
        mask_all = tP[:, 1152:1216]
        pos_all = tP[:, 1216:1280]
        offu64 = tP[:, 1280:1344]
        p8n_t = tP[:, 1344:1408]
        p8_t = tP[:, 1408:1472]
        cmb_all = tP[:, 1472:1536]
        mg8 = tP[:, 1536:1544]
        sg8 = tP[:, 1544:1552]
        m8b = tP[:, 1552:1560]
        viof8 = tP[:, 1560:1568]
        viou8 = tP[:, 1568:1576]
        basetile = tP[:, 1576:1584]
        counts_sb = tM[0:8, 168:176]
        countsT = tM[0:8, 176:184]
        offs_sb = tM[0:8, 184:192]

        def seg(ap, e=8):
            return ap.rearrange("p (g e) -> p g e", e=e)

        def segb(ap8):
            return ap8.rearrange("p (g o) -> p g o", o=1).broadcast_to((128, 8, 8))

        def emit_routing(th):
            combT = combT2[th]
            for cc in range(8):
                xb_ = wfb[cc % 2]
                src2 = xnf[cc * 128:(cc + 1) * 128, :].rearrange("p (b c) -> p b c", c=128)
                src2 = src2[:, 16 * th: 16 * (th + 1): 2, :]
                dma("gp", xb_[0].rearrange("p (b c) -> p b c", c=128), src2,
                    r=xb_[1] + (XNF,), w=xb_[1])
                for tc in range(2):
                    mm(bank(2 + tc)[0:8, :], cGW[:, cc * 8:(cc + 1) * 8],
                       xb_[0][:, tc * 512:(tc + 1) * 512],
                       r=(CB,) + xb_[1], w=(PSB[2 + tc],), start=(cc == 0), stop=(cc == 7))
            for tc in range(2):
                ve_copy(logi[0][:, tc * 512:(tc + 1) * 512], bank(2 + tc)[0:8, :],
                        r=(PSB[2 + tc],), w=(logi[1],))
            for g in range(8):
                tr(bank(7)[:, g * 8:(g + 1) * 8], logi[0][:, g * 128:(g + 1) * 128],
                   cID[0:8, 0:8], r=(logi[1], CB), w=(PSB[7],), start=(g == 0))
            ve_copy(ltok, bank(7)[:, 0:64], r=(PSB[7],), w=gwB)
            P.emit("ve", lambda e: e.tensor_reduce(seg(mg8, 1), seg(ltok), axis=AX.X,
                                                   op=OP.max, negate=True),
                   r=gwB, w=gwB)
            ve_tt(seg(pshift), seg(ltok), segb(mg8), OP.add, r=gwB, w=gwB)
            act(p8_t, pshift, AF.Exp, r=gwB, w=gwB)
            P.emit("ve", lambda e: e.tensor_reduce(seg(sg8, 1), seg(p8_t), axis=AX.X,
                                                   op=OP.add), r=gwB, w=gwB)
            ve_recip(sg8, sg8, r=gwB, w=gwB)
            ve_tt(seg(p8n_t), seg(p8_t), segb(sg8), OP.mult, r=gwB, w=gwB)
            P.emit("ve", lambda e: e.tensor_reduce(seg(m8b, 1), seg(p8n_t), axis=AX.X,
                                                   op=OP.max), r=gwB, w=gwB)
            ve_tt(seg(pshift), seg(p8n_t), segb(m8b), OP.is_ge, r=gwB, w=gwB)
            ve_stt(p8_t, pshift, -BIG, p8n_t, OP.mult, OP.add, r=gwB, w=gwB)
            P.emit("ve", lambda e: e.tensor_reduce(seg(m8b, 1), seg(p8_t), axis=AX.X,
                                                   op=OP.max), r=gwB, w=gwB)
            ve_tt(seg(p8_t), seg(p8_t), segb(m8b), OP.is_ge, r=gwB, w=gwB)
            ve_tt(ltok, pshift, p8_t, OP.add, r=gwB, w=gwB)
            ve_tt(cmb_all, ltok, p8n_t, OP.mult, r=gwB, w=gwB)
            dma("gp", combd[th * 1024:(th + 1) * 1024, :].rearrange("(g p) e -> p g e", p=128),
                seg(cmb_all), r=gwB, w=(CMD,), stream="gI")
            for g in range(8):
                tr(bank(6 if g < 4 else 5)[0:8, (g % 4) * 128:(g % 4 + 1) * 128],
                   cmb_all[:, g * 8:(g + 1) * 8], cID[:, :],
                   r=gwB + (CB,), w=(PSB[6 if g < 4 else 5],), start=(g % 4 == 0))
            ve_copy(combT[0][:, 0:512], bank(6)[0:8, :], r=(PSB[6],), w=(combT[1],))
            ve_copy(combT[0][:, 512:1024], bank(5)[0:8, :], r=(PSB[5],), w=(combT[1],))
            ve_ts(mask_all, cmb_all, 0.0, OP.is_gt, r=gwB, w=gwB)
            for g in range(8):
                bk = 7 if g < 4 else 4
                mm(bank(bk)[0:8, (g % 4) * 128:(g % 4 + 1) * 128],
                   mask_all[:, g * 8:(g + 1) * 8], cTRI[:, 0:128],
                   r=gwB + (CB,), w=(PSB[bk],), start=(g % 4 == 0))
            ve_copy(cums[0][:, 0:512], bank(7)[0:8, :], r=(PSB[7],), w=(cums[1],))
            ve_copy(cums[0][:, 512:1024], bank(4)[0:8, :], r=(PSB[4],), w=(cums[1],))
            ve_copy(counts_sb,
                    cums[0].rearrange("e (g t) -> e g t", t=128)[:, :, 127],
                    r=(cums[1],), w=(MB,))
            tr(bank(7)[0:8, 0:8], counts_sb, cID[0:8, 0:8], r=(MB, CB), w=(PSB[7],))
            ve_copy(countsT, bank(7)[0:8, 0:8], r=(PSB[7],), w=(MB,))
            mm(bank(7)[0:8, 0:8], countsT, cTRI[0:8, 128:136], r=(MB, CB), w=(PSB[7],))
            ve_copy(offs_sb, bank(7)[0:8, 0:8], r=(PSB[7],), w=(MB,))
            for e_ in range(NEXP):
                lb = (th * NEXP + e_) * 640
                dma("sy", listsd[lb:lb + 640, 0].rearrange("(p c) -> p c", c=5),
                    cSENT[:, :], r=(CB,), w=(LB[e_], PFH[th][e_]), stream="sC",
                    war_only=True)
                ve_memset(basetile[:, e_:e_ + 1], float(lb), w=gwB)
            P.emit("ve", lambda e: e.tensor_tensor(
                cums[0].rearrange("e (g t) -> e g t", t=128),
                cums[0].rearrange("e (g t) -> e g t", t=128),
                offs_sb.rearrange("e (g o) -> e g o", o=1).broadcast_to((8, 8, 128)),
                OP.add), r=(cums[1], MB), w=(cums[1],))
            for g in range(8):
                bk = 7 if g < 4 else 4
                tr(bank(bk)[:, (g % 4) * 8:(g % 4 + 1) * 8],
                   cums[0][:, g * 128:(g + 1) * 128], cID[0:8, 0:8],
                   r=(cums[1], CB), w=(PSB[bk],), start=(g % 4 == 0))
            ve_copy(pos_all[:, 0:32], bank(7)[:, 0:32], r=(PSB[7],), w=gwB)
            ve_copy(pos_all[:, 32:64], bank(4)[:, 0:32], r=(PSB[4],), w=gwB)
            ve_ts(ltok, pos_all, -1.0, OP.add, r=gwB, w=gwB, s2=float(CCAP), op1=OP.min)
            ve_tt(pshift, ltok, padf.broadcast_to((128, 64)), OP.subtract,
                  r=gwB + (MB,), w=gwB)
            ve_tt(pshift, pshift, mask_all, OP.mult, r=gwB, w=gwB)
            ve_tt(pshift, pshift, padf.broadcast_to((128, 64)), OP.add,
                  r=gwB + (MB,), w=gwB)
            P.emit("ve", lambda e: e.tensor_tensor(
                seg(pshift), seg(pshift),
                basetile.rearrange("p (o e) -> p o e", o=1).broadcast_to((128, 8, 8)),
                OP.add), r=gwB, w=gwB)
            ve_copy(offu64.bitcast(mybir.dt.uint32), pshift, r=gwB, w=gwB)
            for g in range(8):
                ve_ts(viof8[:, g:g + 1], cI128[:, :], float(th * 1024 + g * 128), OP.add,
                      r=(CB,), w=gwB)
            ve_copy(viou8.bitcast(mybir.dt.uint32), viof8, r=gwB, w=gwB)
            for g in range(8):
                for e_ in range(NEXP):
                    idma_scatter(listsd, viou8.bitcast(mybir.dt.uint32)[:, g:g + 1],
                                 offu64.bitcast(mybir.dt.uint32)[:, g * 8 + e_: g * 8 + e_ + 1],
                                 r=gwB + (PFH[th][e_],), w=(LB[e_],), war_only=True)
            if "comb" in tapd:
                dma("gp", tapd["comb"][:, th * 1024:(th + 1) * 1024], combT[0],
                    r=(combT[1],), w=())

        wo_tiles = []
        for cc in range(8):
            if cc < 4:
                ap, bref = tK[:, cc * 1024:(cc + 1) * 1024], KB[cc]
            else:
                ap, bref = tV[:, (cc - 4) * 1024:(cc - 3) * 1024], VB[cc - 4]
            wo_tiles.append((ap, bref))
            dma("sy", ap, woT[cc * 128:(cc + 1) * 128, :], r=(bref,), w=(bref,))

        def emit_oproj(dcs):
            for dc in dcs:
                for tc in range(4):
                    pbk = PSB[tc % 2]
                    for cc in range(8):
                        mm(bank(tc % 2), wo_tiles[cc][0][:, dc * 128:(dc + 1) * 128],
                           tO[:, cc * 2048 + tc * 512: cc * 2048 + (tc + 1) * 512],
                           r=(wo_tiles[cc][1], OB[cc]), w=(pbk,),
                           start=(cc == 0), stop=(cc == 7))
                    ve_copy(acc[dc][:, tc * 512:(tc + 1) * 512], bank(tc % 2),
                            r=(pbk,), w=(accB[dc],))

        emit_routing(0)
        emit_oproj(range(0, 3))
        emit_routing(1)
        emit_oproj(range(3, 8))
        for th_ in range(2):
            for dc in range(8):
                for tc in range(2):
                    mm(bank(2 + tc), cB2[0:8, dc * 128:(dc + 1) * 128],
                       combT2[th_][0][:, tc * 512:(tc + 1) * 512],
                       r=(WB[2], combT2[th_][1]), w=(PSB[2 + tc],))
                    aslc = acc[dc][:, th_ * 1024 + tc * 512: th_ * 1024 + (tc + 1) * 512]
                    ve_tt(aslc, aslc, bank(2 + tc), OP.add,
                          r=(accB[dc], PSB[2 + tc]), w=(accB[dc],))
        if "accp3" in tapd:
            for dc in range(8):
                dma("gp", tapd["accp3"][dc * 128:(dc + 1) * 128, :], acc[dc],
                    r=(accB[dc],), w=())
        P.barrier()

        xnh = []
        for cc in range(8):
            if cc < 4:
                xnh.append((tK[:, cc * 1024:(cc + 1) * 1024], KB[cc]))
            else:
                xnh.append((tV[:, (cc - 4) * 1024:(cc - 3) * 1024], VB[cc - 4]))

        for th in range(2):
            U32 = mybir.dt.uint32
            lcol2 = [tP[:, 1592:1595], tP[:, 1600:1603]]
            gcol2 = [tP[:, 1596:1599], tP[:, 1604:1607]]
            cgt2 = [[tO[:, 15360 + i * 8: 15368 + i * 8] for i in range(3)],
                    [tO[:, 15384 + i * 8: 15392 + i * 8] for i in range(3)]]

            def emit_route_load(e):
                sl = e % 2
                lb = (th * NEXP + e) * 640
                dma("gp", lcol2[sl].bitcast(U32),
                    listsd[lb:lb + CCAP, 0].rearrange("(p c) -> p c", c=3),
                    r=(LB[e], PB[3]), w=(PB[3],), stream="gI")
                ve_ts(gcol2[sl].bitcast(U32), lcol2[sl].bitcast(U32),
                      float(TOWN - 1), OP.min, r=(PB[3],), w=(PB[3],))
                for i in range(3):
                    idma_gather(xgt[i], xntd, gcol2[sl].bitcast(U32)[:, i:i + 1],
                                r=(XNT, PB[3]) + ob(6144 + i * 1024, 6144 + (i + 1) * 1024),
                                w=ob(6144 + i * 1024, 6144 + (i + 1) * 1024))
                    idma_gather(cgt2[sl][i], combd, lcol2[sl].bitcast(U32)[:, i:i + 1],
                                r=(CMD, PB[3]) + ob(15360, 15408),
                                w=ob(15360 + sl * 24 + i * 8, 15368 + sl * 24 + i * 8))

            def emit_xg_tr(e):
                for i in range(3):
                    for cc in range(8):
                        bk = 4 + (i * 8 + cc) % 2
                        tr(bank(bk)[:, 0:128], xgt[i][:, cc * 128:(cc + 1) * 128], cID[:, :],
                           r=ob(6144 + i * 1024, 6144 + (i + 1) * 1024) + (CB,), w=(PSB[bk],))
                        ve_copy(xgf[cc][:, i * 128:(i + 1) * 128], bank(bk)[:, 0:128],
                                r=(PSB[bk],),
                                w=ob(9216 + cc * CCAP, 9216 + (cc + 1) * CCAP))

            emit_route_load(0)
            for cc in range(8):
                src = xnf[cc * 128:(cc + 1) * 128, :].rearrange("p (b c) -> p b c", c=128)
                src = src[:, 16 * th: 16 * (th + 1): 2, :]
                dma("gp", xnh[cc][0].rearrange("p (b c) -> p b c", c=128), src,
                    r=(xnh[cc][1], XNF), w=(xnh[cc][1],))
            emit_xg_tr(0)
            for e in range(NEXP):
                sl = e % 2
                for mat, hdst, is1 in ((w1[e], h1sp, True), (w3[e], h3sp, False)):
                    for ic in range(8):
                        wb_ = wfb[ic % 2]
                        dma("sy", wb_[0].rearrange("p (c d) -> p c d", d=128),
                            mat[:, ic * 128:(ic + 1) * 128].rearrange("(c p) d -> p c d", p=128),
                            r=wb_[1], w=wb_[1])
                        pb2 = (2 if not is1 else 0) + (ic % 2)
                        for cc in range(8):
                            mm(bank(pb2)[:, 0:CCAP], wb_[0][:, cc * 128:(cc + 1) * 128],
                               xgf[cc],
                               r=wb_[1] + ob(9216 + cc * CCAP, 9216 + (cc + 1) * CCAP),
                               w=(PSB[pb2],), start=(cc == 0), stop=(cc == 7))
                        bias_ap = (cB1 if is1 else cB3)[:, e * 8 + ic: e * 8 + ic + 1]
                        base = 0 if is1 else 3072
                        act(hdst[ic], bank(pb2)[:, 0:CCAP],
                            AF.Silu if is1 else AF.Identity,
                            r=(PSB[pb2], CB),
                            w=ob(base + ic * CCAP, base + (ic + 1) * CCAP), bias=bias_ap)
                if e + 1 < NEXP:
                    emit_route_load(e + 1)
                for ic in range(8):
                    ve_tt(h1sp[ic], h1sp[ic], h3sp[ic], OP.mult,
                          r=ob(ic * CCAP, (ic + 1) * CCAP) + ob(3072 + ic * CCAP, 3072 + (ic + 1) * CCAP),
                          w=ob(ic * CCAP, (ic + 1) * CCAP))
                for dc in range(8):
                    wb_ = wfb[dc % 2]
                    dma("sy", wb_[0].rearrange("p (c d) -> p c d", d=128),
                        w2[e][:, dc * 128:(dc + 1) * 128].rearrange("(c p) d -> p c d", p=128),
                        r=wb_[1], w=wb_[1])
                    pb2 = dc % 2
                    for ic in range(8):
                        mm(bank(pb2)[:, 0:CCAP], wb_[0][:, ic * 128:(ic + 1) * 128],
                           h1sp[ic], r=wb_[1] + ob(ic * CCAP, (ic + 1) * CCAP),
                           w=(PSB[pb2],), start=(ic == 0), stop=(ic == 7))
                    ve_copy(h3sp[dc], bank(pb2)[:, 0:CCAP], r=(PSB[pb2],),
                            w=ob(3072 + dc * CCAP, 3072 + (dc + 1) * CCAP))
                if e + 1 < NEXP:
                    emit_xg_tr(e + 1)
                for i in range(3):
                    for dc in range(8):
                        bk = 6 + dc % 2
                        tr(bank(bk)[:, 0:128], h3sp[dc][:, i * 128:(i + 1) * 128],
                           cID[:, :],
                           r=ob(3072 + dc * CCAP, 3072 + (dc + 1) * CCAP) + (CB,),
                           w=(PSB[bk],), start=True)
                        ve_copy(ztok[i][:, dc * 128:(dc + 1) * 128], bank(bk)[:, 0:128],
                                r=(PSB[bk],),
                                w=ob(12288 + i * 1024, 12288 + (i + 1) * 1024))
                    ve_tt(ztok[i], ztok[i],
                          cgt2[sl][i][:, e:e + 1].broadcast_to((128, 1024)),
                          OP.mult,
                          r=ob(12288 + i * 1024, 12288 + (i + 1) * 1024)
                            + ob(15360 + sl * 24 + i * 8, 15368 + sl * 24 + i * 8),
                          w=ob(12288 + i * 1024, 12288 + (i + 1) * 1024))
                    idma_scatter(yffn, ztok[i], lcol2[sl].bitcast(U32)[:, i:i + 1],
                                 r=ob(12288 + i * 1024, 12288 + (i + 1) * 1024) + (PB[3], YFB),
                                 w=(YFB,), accum=True)

            for e in (NEXP,):
                wsrc1, wsrc3, wsrc2 = sw1, sw3, sw2
                for mat, hdst, fn_, bias_ in ((wsrc1, h1s, AF.Silu, cSB1),
                                              (wsrc3, h3s, AF.Copy, cSB3)):
                    for ic in range(8):
                        wb_ = wfb[ic % 2]
                        dma("sy", wb_[0].rearrange("p (c d) -> p c d", d=128),
                            mat[:, ic * 128:(ic + 1) * 128].rearrange("(c p) d -> p c d", p=128),
                            r=wb_[1], w=wb_[1])
                        pb2 = (4 if mat is wsrc3 else 0) + (ic % 2) * 2
                        for cc in range(8):
                            for tc in range(2):
                                mm(bank(pb2 + tc), wb_[0][:, cc * 128:(cc + 1) * 128],
                                   xnh[cc][0][:, tc * 512:(tc + 1) * 512],
                                   r=wb_[1] + (xnh[cc][1],), w=(PSB[pb2 + tc],),
                                   start=(cc == 0), stop=(cc == 7))
                        bias_ap = bias_[:, ic:ic + 1]
                        for tc in range(2):
                            act(h1s[ic][0][:, tc * 512:(tc + 1) * 512] if fn_ == AF.Silu
                                else h3s[ic][0][:, tc * 512:(tc + 1) * 512],
                                bank(pb2 + tc),
                                AF.Silu if fn_ == AF.Silu else AF.Identity,
                                r=(PSB[pb2 + tc], CB),
                                w=((h1s[ic][1],) if fn_ == AF.Silu else (h3s[ic][1],)),
                                bias=bias_ap)
                for ic in range(8):
                    ve_tt(h1s[ic][0], h1s[ic][0], h3s[ic][0], OP.mult,
                          r=(h1s[ic][1], h3s[ic][1]), w=(h1s[ic][1],))
                for dc in range(8):
                    wb_ = wfb[dc % 2]
                    dma("sy", wb_[0].rearrange("p (c d) -> p c d", d=128),
                        wsrc2[:, dc * 128:(dc + 1) * 128].rearrange("(c p) d -> p c d", p=128),
                        r=wb_[1], w=wb_[1])
                    pb2 = (dc % 2) * 2
                    for ic in range(8):
                        for tc in range(2):
                            mm(bank(pb2 + tc), wb_[0][:, ic * 128:(ic + 1) * 128],
                               h1s[ic][0][:, tc * 512:(tc + 1) * 512],
                               r=wb_[1] + (h1s[ic][1],), w=(PSB[pb2 + tc],),
                               start=(ic == 0), stop=(ic == 7))
                    for tc in range(2):
                        aslc = acc[dc][:, th * 1024 + tc * 512: th * 1024 + (tc + 1) * 512]
                        act(tmpz[0], bank(pb2 + tc), AF.Identity,
                            r=(PSB[pb2 + tc], CB), w=(tmpz[1],), bias=cSB2[:, dc:dc + 1])
                        ve_tt(aslc, aslc, tmpz[0], OP.add,
                              r=(accB[dc], tmpz[1]), w=(accB[dc],))
        P.barrier()

        outrow = [(tO[:, 0:1024], OB[0]), (tO[:, 1024:2048], OB[4])]
        xyrow = [(tO[:, 2048:3072], OB[1]), (tO[:, 3072:4096], OB[5])]
        xrow2 = [(tO[:, 4096:5120], OB[2]), (tO[:, 5120:6144], OB[6])]
        yfr2 = [(tO[:, 6144:7168], OB[3]), (tO[:, 7168:8192], OB[7])]
        for jp in range(NOWN // 2):
            for j in (2 * jp, 2 * jp + 1):
                sl = j % 2
                xr, xrB = xrow2[sl]
                yr, yrB = yfr2[sl]
                dma("gp", xr, x[2 * j * 128:(2 * j + 1) * 128, :], r=(xrB,), w=(xrB,))
                dma("gp", yr, yffn[j * 128:(j + 1) * 128, :], r=(YFB, yrB), w=(yrB,))
            for j in (2 * jp, 2 * jp + 1):
                sl = j % 2
                xy, xyB = xyrow[sl]
                xr, xrB = xrow2[sl]
                yr, yrB = yfr2[sl]
                ps = PS[j % 2]
                psb = (PSB[2 * (j % 2)], PSB[2 * (j % 2) + 1])
                ve_tt(xy, xr, yr, OP.add, r=(xrB, yrB, xyB), w=(xyB,))
                for dc in range(8):
                    tr(ps[:, dc * 128:(dc + 1) * 128], acc[dc][:, j * 128:(j + 1) * 128],
                       cID[:, :], r=(accB[dc], CB), w=(psb[dc // 4],), start=(dc % 4 == 0))
            for j in (2 * jp, 2 * jp + 1):
                sl = j % 2
                orow, orB = outrow[sl]
                xy, xyB = xyrow[sl]
                ps = PS[j % 2]
                psb = (PSB[2 * (j % 2)], PSB[2 * (j % 2) + 1])
                ve_tt(orow, xy, ps[:, :1024], OP.add, r=psb + (xyB, orB), w=(orB,))
                dma("gp", y[j * 128:(j + 1) * 128, :], orow, r=(orB,), w=(orB,))
        P.barrier()

        block = es.enter_context(nc.Block())
        eng_attr = {"te": "tensor", "ve": "vector", "se": "scalar",
                    "sy": "sync", "gp": "gpsimd"}
        for ename in ENGS:
            opl = P.ops[ename]
            if not opl:
                continue

            def body(eng, _opl=opl, _en=ename):
                for waits, fn, step, tag in _opl:
                    for dsem, v in waits:
                        eng.wait_ge(sem[dsem], v)
                    if fn is not None:
                        ins = fn(eng)
                        if step:
                            ins.then_inc(sem[tag], step)
            getattr(block, eng_attr[ename])(body)
    return nc


def _host_inputs(inp):
    x = np.asarray(inp["L0_attn_x"], dtype=np.float32)
    com = {
        "wqT": np.ascontiguousarray(np.asarray(inp["L0_attn_Wq"], np.float32).T),
        "wkT": np.ascontiguousarray(np.asarray(inp["L0_attn_Wk"], np.float32).T),
        "wvT": np.ascontiguousarray(np.asarray(inp["L0_attn_Wv"], np.float32).T),
        "woT": np.ascontiguousarray(np.asarray(inp["L0_attn_Wo"], np.float32).T),
        "gwT": np.ascontiguousarray(np.asarray(inp["L0_ffn_gate_w"], np.float32).T),
        "w1": np.ascontiguousarray(np.asarray(inp["L0_ffn_w1"], np.float32)),
        "w2": np.ascontiguousarray(np.asarray(inp["L0_ffn_w2"], np.float32)),
        "w3": np.ascontiguousarray(np.asarray(inp["L0_ffn_w3"], np.float32)),
        "b2": np.ascontiguousarray(np.asarray(inp["L0_ffn_b2"], np.float32)),
        "sw1": np.ascontiguousarray(np.asarray(inp["L0_ffn_sw1"], np.float32)),
        "sw2": np.ascontiguousarray(np.asarray(inp["L0_ffn_sw2"], np.float32)),
        "sw3": np.ascontiguousarray(np.asarray(inp["L0_ffn_sw3"], np.float32)),
    }
    for nm, key in (("b1c", "L0_ffn_b1"), ("b3c", "L0_ffn_b3")):
        bmat = np.asarray(inp[key], np.float32)
        com[nm] = np.ascontiguousarray(
            bmat.reshape(8, 8, 128).transpose(2, 0, 1).reshape(128, 64))
    for nm, key in (("sb1c", "L0_ffn_sb1"), ("sb2c", "L0_ffn_sb2"), ("sb3c", "L0_ffn_sb3")):
        v = np.asarray(inp[key], np.float32)
        com[nm] = np.ascontiguousarray(v.reshape(8, 128).T)
    com["ident"] = np.eye(128, dtype=np.float32)
    ctri = np.zeros((128, 144), dtype=np.float32)
    ctri[:, 0:128] = (np.arange(128)[:, None] <= np.arange(128)[None, :])
    ctri[0:16, 128:144] = (np.arange(16)[:, None] < np.arange(16)[None, :])
    com["ctri"] = ctri
    com["ci128"] = np.arange(128, dtype=np.float32).reshape(128, 1)
    con2 = np.zeros((128, 2), dtype=np.float32)
    con2[0:64, 0] = 1.0
    con2[64:128, 1] = 1.0
    com["cones2"] = con2
    csl2 = np.zeros((2, 256), dtype=np.float32)
    csl2[0, 0:128] = 1.0
    csl2[1, 128:256] = 1.0
    com["csel2"] = csl2
    cslp = np.zeros((2, NEXP), dtype=np.float32)
    for p_ in range(NEXP):
        cslp[0, p_] = SLOPES[2 * p_]
        cslp[1, p_] = SLOPES[2 * p_ + 1]
    com["cslopes"] = cslp
    com["bco"] = np.ones((1, 128), dtype=np.float32)
    com["onescol"] = np.ones((128, 1), dtype=np.float32)
    tri = np.where(np.arange(128)[:, None] > np.arange(128)[None, :], BIG, 0.0).astype(np.float32)
    in_maps = []
    for core in range(8):
        b, par = core // 2, core % 2
        m = dict(com)
        xb = x[b].reshape(NBLK, 128, C)
        if par == 1:
            perm = np.arange(NBLK).reshape(NBLK // 2, 2)[:, ::-1].reshape(NBLK)
            xb = xb[perm]
        m["x"] = np.ascontiguousarray(xb.reshape(T, C))
        tb = np.arange(NBLK) ^ par
        ali = np.zeros((128, 512), dtype=np.float32)
        for h in range(H):
            for sc in range(NBLK):
                ali[:, h * 32 + sc] = SLOPES[h] * (tb[sc] * 128 + np.arange(128))
        m["alibi"] = ali
        m["maska"] = tri
        m["maskb"] = (np.full((128, 128), BIG, dtype=np.float32) if par == 0
                      else np.zeros((128, 128), dtype=np.float32))
        iota = np.zeros((1, TOWN), dtype=np.float32)
        for j in range(NOWN):
            iota[0, j * 128:(j + 1) * 128] = (2 * j + par) * 128 + np.arange(128)
        m["iota"] = iota
        in_maps.append(m)
    return in_maps


def kernel(**inp):
    in_maps = _host_inputs(inp)
    nc = _build()
    res = run_bass_kernel_spmd(nc, in_maps, core_ids=list(range(8)))
    out = np.zeros((B, T, C), dtype=np.float32)
    for core in range(8):
        b, par = core // 2, core % 2
        yc = np.asarray(res.results[core]["y"]).reshape(NOWN, 128, C)
        for j in range(NOWN):
            out[b, (2 * j + par) * 128:(2 * j + par + 1) * 128, :] = yc[j]
    return out
